# version 14 (final)
# baseline (speedup 1.0000x reference)
import contextlib
import math

import numpy as np

import concourse.bass as bass
import concourse.mybir as mybir
from concourse.bass_utils import run_bass_kernel_spmd

F32 = mybir.dt.float32
BF16 = mybir.dt.bfloat16
I32 = mybir.dt.int32
AF = mybir.ActivationFunctionType
ALU = mybir.AluOpType
AX = mybir.AxisListType

D = 4096
S = 4096
TOWN = 2048
NHO = 16
PAN = 512
NPAN = S // PAN
QT = 1024
NQT = S // QT
NSC = S // 128
NIC = D // 128
NOC = D // 128
NQUAD = 4
TQ = TOWN // NQUAD
ERC = 8
TWO_PI = 2.0 * math.pi
RSQRT_DH = 1.0 / math.sqrt(128.0)
LOG_BASE = math.log(10000.0)
EPS = 1e-5
import numpy as _np
CW1 = float(_np.float32(6.28125))
CW2 = float(_np.float32(2.0 * math.pi - 6.28125))
CW3 = float(2.0 * math.pi - float(_np.float32(6.28125)) -
            float(_np.float32(2.0 * math.pi - 6.28125)))


class Sem:
    def __init__(self, nc, es, name):
        self.h = es.enter_context(nc.semaphore(name))
        self.n = 0

    def inc(self, instr, k=1):
        instr.then_inc(self.h, k)
        self.n += k
        return self.n

    def wait(self, eng, n=None):
        eng.wait_ge(self.h, self.n if n is None else n)


class P:
    pass


def build(phases=("init", "a", "b", "c", "de"), dump=()):
    nc = bass.Bass("TRN2", target_bir_lowering=False, debug=False)
    p = P()
    p.nc = nc
    p.dump = set(dump)
    p.final_waits = []

    p.xb = nc.dram_tensor("xb", [S, D], F32, kind="ExternalInput").ap()
    p.tpos = nc.dram_tensor("tpos", [1, S], F32, kind="ExternalInput").ap()
    p.gsel = nc.dram_tensor("gsel", [1, 2], I32, kind="ExternalInput").ap()
    p.invf = nc.dram_tensor("invf", [128, 1], F32, kind="ExternalInput").ap()
    p.wqT = nc.dram_tensor("wqT", [D, 2048], F32, kind="ExternalInput").ap()
    p.wkT = nc.dram_tensor("wkT", [D, 2048], F32, kind="ExternalInput").ap()
    p.wvT = nc.dram_tensor("wvT", [D, 2048], F32, kind="ExternalInput").ap()
    p.woT = nc.dram_tensor("woT", [D, D], F32, kind="ExternalInput").ap()
    p.bwT = nc.dram_tensor("bwT", [D, D], F32, kind="ExternalInput").ap()
    p.bb = nc.dram_tensor("bb", [1, D], F32, kind="ExternalInput").ap()
    p.art = nc.dram_tensor("art", [D, 80], F32, kind="ExternalInput").ap()
    p.btT = nc.dram_tensor("btT", [1024, D], F32, kind="ExternalInput").ap()

    p.out = nc.dram_tensor("out", [TOWN, D], F32, kind="ExternalOutput").ap()

    p.xbf = nc.dram_tensor("xbf", [S, D], BF16, kind="Internal").ap()
    p.xf = nc.dram_tensor("xf", [NIC, 128, S], BF16, kind="Internal").ap()
    p.qf = nc.dram_tensor("qf", [NPAN, NHO, 128, PAN], BF16, kind="Internal").ap()
    p.kf = nc.dram_tensor("kf", [NPAN, NHO, 128, PAN], BF16, kind="Internal").ap()
    p.vf = nc.dram_tensor("vf", [NPAN, NHO, 128, PAN], BF16, kind="Internal").ap()
    p.ao_keep = nc.dram_tensor("ao_keep", [NHO, 128, TOWN], BF16, kind="Internal").ap()
    p.ao_send = nc.dram_tensor("ao_send", [NHO, 128, TOWN], BF16, kind="Internal").ap()
    p.ao_gath = nc.dram_tensor("ao_gath", [2, NHO, 128, TOWN], BF16, kind="Internal").ap()
    p.ao_part = nc.dram_tensor("ao_part", [NHO, 128, TOWN], BF16, kind="Internal").ap()
    p.wo_bf = nc.dram_tensor("wo_bf", [D, D], BF16, kind="Internal").ap()
    p.bw_bf = nc.dram_tensor("bw_bf", [D, D], BF16, kind="Internal").ap()
    p.bt_bf = nc.dram_tensor("bt_bf", [1024, D], BF16, kind="Internal").ap()

    with contextlib.ExitStack() as es:
        p.es = es
        p.ones_bf = es.enter_context(nc.sbuf_tensor("ones_bf", [128, 1], BF16))
        p.ones1_f = es.enter_context(nc.sbuf_tensor("ones1_f", [1, 128], F32))
        p.ones512_bf = es.enter_context(nc.sbuf_tensor("ones512_bf", [1, 512], BF16))
        p.id_f = es.enter_context(nc.sbuf_tensor("id_f", [128, 128], F32))
        p.id_bf = es.enter_context(nc.sbuf_tensor("id_bf", [128, 128], BF16))
        p.sel = es.enter_context(nc.sbuf_tensor("sel", [16, ERC * 128], BF16))
        p.bb_bf = es.enter_context(nc.sbuf_tensor("bb_bf", [1, D], BF16))
        p.eps_sb = es.enter_context(nc.sbuf_tensor("eps_sb", [128, 1], F32))
        p.ones128_f = es.enter_context(nc.sbuf_tensor("ones128_f", [128, 1], F32))
        p.gs_sb = es.enter_context(nc.sbuf_tensor("gs_sb", [1, 2], I32))

        p.a_sem = Sem(nc, es, "a_sem")

        with contextlib.ExitStack() as bscope:
            p.cos = bscope.enter_context(nc.sbuf_tensor("cos", [128, S], F32))
            p.sina = bscope.enter_context(nc.sbuf_tensor("sina", [128, S], F32))
            if "a" in phases:
                emit_a(p)
            if "init" in phases:
                emit_init(p)
            if "a" in phases:
                emit_a2(p)
            if "init" in phases:
                nc.all_engine_barrier()
            if "b" in phases:
                emit_b(p)
                nc.all_engine_barrier()
        if "c" in phases:
            emit_c(p)
            nc.all_engine_barrier()
        if "de" in phases:
            emit_de(p)
        if p.dump:
            nc.all_engine_barrier()
            emit_dumps(p)
        emit_final_wait(p)
    return nc



def emit_init(p):
    import os
    lvl = int(os.environ.get("MK_INIT_LVL", "3"))
    nc = p.nc
    with contextlib.ExitStack() as s:
        iof = s.enter_context(nc.sbuf_tensor("iof", [128, S], F32))
        sc1 = s.enter_context(nc.sbuf_tensor("sc1", [128, S], F32))
        sc2 = s.enter_context(nc.sbuf_tensor("sc2", [128, S], F32))
        sc3 = s.enter_context(nc.sbuf_tensor("sc3", [128, S], F32))
        pp = s.enter_context(nc.sbuf_tensor("pp", [128, 1], F32))
        invf = s.enter_context(nc.sbuf_tensor("invf_sb", [128, 1], F32))
        tps = s.enter_context(nc.sbuf_tensor("tps", [1, S], F32))
        t1 = s.enter_context(nc.sbuf_tensor("t16_1", [16, 128], F32))
        t2 = s.enter_context(nc.sbuf_tensor("t16_2", [16, 128], F32))
        tpp = s.enter_context(nc.psum_tensor("tpp", [128, QT], F32))
        gs = Sem(nc, s, "ini_g")
        vs = Sem(nc, s, "ini_v")
        ss = Sem(nc, s, "ini_s")
        ts_ = Sem(nc, s, "ini_t")
        ls = Sem(nc, s, "ini_l")

        with nc.Block() as blk:
            @blk.sync
            def _(sy):
                ls.inc(sy.dma_start(out=tps[:, :], in_=p.tpos[:, :]), 16)
                ls.inc(sy.dma_start(out=invf[:, :], in_=p.invf[:, :]), 16)

            @blk.gpsimd
            def _(g):
                g.iota(iof[:, :], [[1, S]], channel_multiplier=0,
                       allow_small_or_imprecise_dtypes=True)
                gs.inc(g.iota(pp[:, :], [[0, 1]], channel_multiplier=1,
                              allow_small_or_imprecise_dtypes=True))

            @blk.vector
            def _(v):
                v.memset(p.ones_bf[:, :], 1.0)
                v.memset(p.eps_sb[:, :], EPS)
                v.memset(p.ones128_f[:, :], 1.0)
                v.memset(p.ones1_f[:, :], 1.0)
                v.memset(p.ones512_bf[:, :], 1.0)
                gs.wait(v)
                vs.inc(v.tensor_scalar(p.id_f[:, :], iof[:, 0:128], pp[:, 0:1],
                                       None, ALU.is_equal))
                vs.wait(v)
                vs.inc(v.tensor_copy(p.id_bf[:, :], p.id_f[:, :]))
                vs.inc(v.tensor_scalar(t1[:, :], iof[0:16, 0:128], 64.0, None,
                                       ALU.is_ge))
                vs.wait(v)
                for c in range(ERC):
                    vs.inc(v.tensor_scalar_add(t2[:, :], t1[:, :], float(2 * c)))
                    vs.wait(v)
                    vs.inc(v.tensor_scalar(p.sel[:, c * 128:(c + 1) * 128],
                                           t2[:, :], pp[0:16, 0:1], 0.5,
                                           ALU.is_equal, ALU.mult))
                    vs.wait(v)
                pass

            @blk.scalar
            def _(scl):
                vs.wait(scl)
                scl.wait_ge(ls.h, 32)
                ss.inc(scl.copy(sc3[:, 0:1], invf[:, :]))

        if lvl < 2:
            return
        NJ = int(os.environ.get("MK_NJ", str(S // QT)))
        with nc.Block() as blk:
            @blk.tensor
            def _(t):
                ls.wait(t, 32)
                vs.wait(t)
                for j in range(NJ):
                    if j >= 1:
                        t.wait_ge(ss.h, 1 + j)
                    t.matmul(tpp[:, 0:512], p.ones1_f[:, :],
                             tps[:, j * QT:j * QT + 512], start=True, stop=True)
                    ts_.inc(t.matmul(tpp[:, 512:QT], p.ones1_f[:, :],
                                     tps[:, j * QT + 512:(j + 1) * QT],
                                     start=True, stop=True))

            @blk.scalar
            def _(scl):
                for j in range(NJ):
                    scl.wait_ge(ts_.h, j + 1)
                    ss.inc(scl.copy(iof[:, j * QT:(j + 1) * QT], tpp[:, :]))

        def tables():
            H = [slice(0, S // 2), slice(S // 2, S)]
            with nc.Block() as blk:
                @blk.vector
                def _(v):
                    ss.wait(v)
                    for hs in H:
                        vs.inc(v.tensor_scalar(sc1[:, hs], iof[:, hs],
                                               invf[:, 0:1], None, ALU.mult))
                    vs.wait(v)
                    for hs in H:
                        vs.inc(v.tensor_copy(p.sina[:, hs], sc1[:, hs]))
                    v.memset(sc2[:, :], 0.0)
                    vs.wait(v)
                    for j in range(9, -1, -1):
                        bj = TWO_PI * (1 << j)
                        for hs in H:
                            vs.inc(v.tensor_scalar(sc3[:, hs], p.sina[:, hs],
                                                   bj, None, ALU.is_ge))
                        vs.wait(v)
                        for hs in H:
                            vs.inc(v.scalar_tensor_tensor(
                                p.sina[:, hs], sc3[:, hs], -bj, p.sina[:, hs],
                                ALU.mult, ALU.add))
                            vs.inc(v.scalar_tensor_tensor(
                                sc2[:, hs], sc3[:, hs], float(1 << j),
                                sc2[:, hs], ALU.mult, ALU.add))
                        vs.wait(v)
                    for cw in (CW1, CW2, CW3):
                        for hs in H:
                            vs.inc(v.scalar_tensor_tensor(
                                sc1[:, hs], sc2[:, hs], -cw, sc1[:, hs],
                                ALU.mult, ALU.add))
                        vs.wait(v)
                    for bnd, per in ((-math.pi, TWO_PI), (math.pi, -TWO_PI)):
                        op = ALU.is_lt if per > 0 else ALU.is_ge
                        for hs in H:
                            vs.inc(v.tensor_scalar(sc3[:, hs], sc1[:, hs],
                                                   bnd, None, op))
                        vs.wait(v)
                        for hs in H:
                            vs.inc(v.scalar_tensor_tensor(
                                sc1[:, hs], sc3[:, hs], per, sc1[:, hs],
                                ALU.mult, ALU.add))
                        vs.wait(v)
                    for hs in H:
                        vs.inc(v.tensor_scalar_add(sc2[:, hs], sc1[:, hs],
                                                   math.pi / 2.0))
                    vs.wait(v)
                    for hs in H:
                        vs.inc(v.tensor_scalar(sc3[:, hs], sc2[:, hs],
                                               math.pi, None, ALU.is_ge))
                    vs.wait(v)
                    for hs in H:
                        vs.inc(v.scalar_tensor_tensor(
                            sc2[:, hs], sc3[:, hs], -TWO_PI, sc2[:, hs],
                            ALU.mult, ALU.add))

                @blk.scalar
                def _(scl):
                    vs.wait(scl)
                    ss.inc(scl.activation(p.sina[:, :], sc1[:, :], AF.Sin))
                    ss.inc(scl.activation(p.cos[:, :], sc2[:, :], AF.Sin))

        tables()

        with nc.Block() as blk:
            @blk.vector
            def _(v):
                ss.wait(v)
                vs.inc(v.tensor_scalar(p.sina[0:64, :], p.sina[0:64, :], -1.0,
                                       None, ALU.mult))



def emit_a(p):
    nc = p.nc
    with nc.Block() as blk:
        @blk.gpsimd
        def _(g):
            for j in range(NPAN):
                p.a_sem.inc(g.dma_start(out=p.xbf[:, j * PAN:(j + 1) * PAN],
                                        in_=p.xb[:, j * PAN:(j + 1) * PAN]), 16)
            p.a_sem.inc(g.dma_start(out=p.bb_bf[:, :], in_=p.bb[:, :]), 16)
            p.a_sem.inc(g.dma_start(out=p.gs_sb[:, :], in_=p.gsel[:, :]), 16)


def emit_a2(p):
    nc = p.nc
    es = p.es
    p.xfs = [Sem(nc, es, "a2_xfs0"), Sem(nc, es, "a2_xfs1")]
    with contextlib.ExitStack() as s:
        stg = [s.enter_context(nc.sbuf_tensor(f"a2_st{i}", [128, S], BF16))
               for i in range(2)]
        tl = [Sem(nc, s, "a2_tl0"), Sem(nc, s, "a2_tl1")]
        with nc.Block() as blk:
            @blk.sync
            def _(sy):
                for ic in range(NIC):
                    par = ic % 2
                    sy.wait_ge(p.a_sem.h, 16 * (ic // 4 + 1))
                    if ic >= 2:
                        sy.wait_ge(p.xfs[par].h, 16 * (ic // 2))
                    tl[par].inc(sy.dma_start(
                        out=stg[par][:, :],
                        in_=p.xbf[:, ic * 128:(ic + 1) * 128],
                        transpose=True), 16)

            @blk.gpsimd
            def _(g):
                for ic in range(NIC):
                    par = ic % 2
                    g.wait_ge(tl[par].h, 16 * (ic // 2 + 1))
                    p.xfs[par].inc(g.dma_start(out=p.xf[ic, :, :],
                                               in_=stg[par][:, :]), 16)



def emit_b(p):
    nc = p.nc

    CHW = 4
    PW = CHW * 128

    with contextlib.ExitStack() as s:
        b_w = [s.enter_context(nc.sbuf_tensor(f"b_w{i}", [128, CHW * D], BF16))
               for i in range(2)]
        b_pan = [s.enter_context(nc.sbuf_tensor(f"b_pan{i}", [128, NIC * PAN],
                                                BF16))
                 for i in range(2)]
        b_stg = [s.enter_context(nc.sbuf_tensor(f"b_stg{i}", [128, CHW * PAN],
                                                BF16))
                 for i in range(2)]
        b_qc = [s.enter_context(nc.sbuf_tensor(f"b_qc{i}", [128, PAN], F32))
                for i in range(2)]
        b_qs = [s.enter_context(nc.sbuf_tensor(f"b_qs{i}", [128, PAN], F32))
                for i in range(2)]
        b_t1 = [s.enter_context(nc.sbuf_tensor(f"b_t1{i}", [128, PAN], F32))
                for i in range(2)]
        b_t2 = [s.enter_context(nc.sbuf_tensor(f"b_t2{i}", [128, PAN], F32))
                for i in range(2)]
        b_ps = [s.enter_context(nc.psum_tensor(f"b_ps{i}", [128, PAN], F32))
                for i in range(2)]

        wl = [Sem(nc, s, "b_wl0"), Sem(nc, s, "b_wl1")]
        pl = [Sem(nc, s, "b_pl0"), Sem(nc, s, "b_pl1")]
        pe = Sem(nc, s, "b_pe")
        ae = Sem(nc, s, "b_ae")
        dt_ = Sem(nc, s, "b_dt")
        ck = Sem(nc, s, "b_ck")
        ckv = Sem(nc, s, "b_ckv")
        st = [Sem(nc, s, "b_sts0"), Sem(nc, s, "b_sts1")]

        passes = []
        for kind, src, dst in (("q", p.wqT, p.qf), ("k", p.wkT, p.kf),
                               ("v", p.wvT, p.vf)):
            for quarter in range(4):
                passes.append((kind, src, dst, quarter))
        NP = len(passes)

        chunks = []
        nae = nck = nckv = 0
        for pi, (kind, src, dst, quarter) in enumerate(passes):
            rope = kind != "v"
            for pn in range(NPAN):
                for oc in range(CHW):
                    c = {"pi": pi, "pn": pn, "oc": oc, "gi": pi * NPAN + pn,
                         "rope": rope, "pe_after": len(chunks) + 1}
                    if rope:
                        nae += 1
                        nck += 1
                    else:
                        nckv += 1
                    c["ae_after"] = nae
                    c["ck_after"] = nck
                    c["ckv_after"] = nckv
                    chunks.append(c)

        def bank_free_wait(eng, ci):
            if ci < 2:
                return
            c2 = chunks[ci - 2]
            if c2["rope"]:
                eng.wait_ge(ae.h, c2["ae_after"])
            else:
                eng.wait_ge(ckv.h, c2["ckv_after"])

        with nc.Block() as blk:
            @blk.gpsimd
            def _(g):
                g.wait_ge(p.xfs[0].h, p.xfs[0].n)
                g.wait_ge(p.xfs[1].h, p.xfs[1].n)
                for pi, (kind, src, dst, quarter) in enumerate(passes):
                    if pi > 1:
                        g.wait_ge(pe.h, (pi - 1) * NPAN * CHW)
                    for oc in range(CHW):
                        wsrc = src[:, quarter * PW + oc * 128:
                                   quarter * PW + (oc + 1) * 128]
                        wsrc = wsrc.rearrange("(c p) o -> p c o", p=128)
                        wdst = b_w[pi % 2][:, oc * D:(oc + 1) * D].rearrange(
                            "q (c o) -> q c o", c=NIC)
                        wl[pi % 2].inc(g.dma_start(out=wdst, in_=wsrc), 16)
                    for pn in range(NPAN):
                        gi = pi * NPAN + pn
                        if kind != "v":
                            g.wait_ge(ck.h, (gi + 1) * CHW)
                        else:
                            g.wait_ge(ckv.h, (gi - 8 * NPAN + 1) * CHW)
                        odst = dst[pn, quarter * CHW:(quarter + 1) * CHW, :, :]
                        st[gi % 2].inc(g.dma_start(
                            out=odst.rearrange("n p t -> p n t"),
                            in_=b_stg[gi % 2][:, :].rearrange(
                                "p (n t) -> p n t", n=CHW)), 16)
                g.wait_ge(st[0].h, st[0].n)
                g.wait_ge(st[1].h, st[1].n)

            @blk.sync
            def _(sy):
                for pi in range(NP):
                    for pn in range(NPAN):
                        gi = pi * NPAN + pn
                        par = gi % 2
                        if gi >= 2:
                            sy.wait_ge(pe.h, (gi - 1) * CHW)
                        if pi == 0 and pn == 0:
                            sy.wait_ge(p.xfs[0].h, p.xfs[0].n)
                            sy.wait_ge(p.xfs[1].h, p.xfs[1].n)
                        for j in range(4):
                            xsrc = p.xf[j * 8:(j + 1) * 8, :,
                                        pn * PAN:(pn + 1) * PAN]
                            pl[par].inc(sy.dma_start(
                                out=b_pan[par][:, j * 8 * PAN:(j + 1) * 8 * PAN]
                                    .rearrange("q (c t) -> q c t", c=8),
                                in_=xsrc.rearrange("c p t -> p c t")), 16)

            @blk.tensor
            def _(t):
                ci = 0
                for pi, (kind, src, dst, quarter) in enumerate(passes):
                    t.wait_ge(wl[pi % 2].h, (pi // 2 + 1) * CHW * 16)
                    for pn in range(NPAN):
                        gi = pi * NPAN + pn
                        t.wait_ge(pl[gi % 2].h, (gi // 2 + 1) * 4 * 16)
                        buf = b_pan[gi % 2]
                        for oc in range(CHW):
                            bank_free_wait(t, ci)
                            for ic in range(NIC):
                                mm = t.matmul(
                                    b_ps[ci % 2][:, :],
                                    b_w[pi % 2][:, oc * D + ic * 128:
                                                oc * D + (ic + 1) * 128],
                                    buf[:, ic * PAN:(ic + 1) * PAN],
                                    start=(ic == 0), stop=(ic == NIC - 1))
                            pe.inc(mm)
                            ci += 1

            @blk.scalar
            def _(scl):
                for ci, c in enumerate(chunks):
                    scl.wait_ge(pe.h, c["pe_after"])
                    if c["rope"]:
                        if ci >= 2 and chunks[ci - 2]["rope"]:
                            scl.wait_ge(dt_.h, chunks[ci - 2]["ae_after"])
                        scl.copy(b_qc[ci % 2][:, :], b_ps[ci % 2][:, :])
                        scl.copy(b_qs[ci % 2][0:64, :], b_ps[ci % 2][64:128, :])
                        ae.inc(scl.copy(b_qs[ci % 2][64:128, :],
                                        b_ps[ci % 2][0:64, :]))
                    else:
                        if c["gi"] >= 2 and c["oc"] == 0:
                            scl.wait_ge(st[c["gi"] % 2].h, (c["gi"] // 2) * 16)
                        ckv.inc(scl.copy(
                            b_stg[c["gi"] % 2][:, c["oc"] * PAN:(c["oc"] + 1) * PAN],
                            b_ps[ci % 2][:, :]))

            @blk.vector
            def _(v):
                for ci, c in enumerate(chunks):
                    if not c["rope"]:
                        continue
                    par = ci % 2
                    gi, pn, oc = c["gi"], c["pn"], c["oc"]
                    tsl = slice(pn * PAN, (pn + 1) * PAN)
                    v.wait_ge(ae.h, c["ae_after"])
                    if ci >= 2 and chunks[ci - 2]["rope"]:
                        v.wait_ge(ck.h, chunks[ci - 2]["ck_after"])
                    qc = b_qc[par]
                    v.tensor_tensor(b_t1[par][:, :], qc[:, :],
                                    p.cos[:, tsl], ALU.mult)
                    dt_.inc(v.tensor_tensor(b_t2[par][:, :], b_qs[par][:, :],
                                            p.sina[:, tsl], ALU.mult))
                    v.wait_ge(dt_.h, c["ae_after"])
                    if gi >= 2 and oc == 0:
                        v.wait_ge(st[gi % 2].h, (gi // 2) * 16)
                    ck.inc(v.tensor_tensor(
                        b_stg[gi % 2][:, oc * PAN:(oc + 1) * PAN],
                        b_t1[par][:, :], b_t2[par][:, :], ALU.add))

        p.final_waits.append((st[0], st[0].n))
        p.final_waits.append((st[1], st[1].n))



CQT = 512
CNQT = S // CQT


def emit_c(p):
    nc = p.nc
    es = p.es

    with contextlib.ExitStack() as s:
        q_sb = [s.enter_context(nc.sbuf_tensor(f"c_q{i}", [128, S], BF16))
                for i in range(2)]
        k_sb = [s.enter_context(nc.sbuf_tensor(f"c_k{i}", [128, S], BF16))
                for i in range(2)]
        v_sb = [s.enter_context(nc.sbuf_tensor(f"c_v{i}", [128, S], BF16))
                for i in range(2)]
        e_sb = [s.enter_context(nc.sbuf_tensor(f"c_e{i}", [128, CQT], BF16))
                for i in range(8)]
        ao_sb = [s.enter_context(nc.sbuf_tensor(f"c_ao{i}", [128, CQT], BF16))
                 for i in range(2)]
        accd = [s.enter_context(nc.sbuf_tensor(f"c_accd{i}", [128, CQT], F32))
                for i in range(2)]
        accp = [s.enter_context(nc.sbuf_tensor(f"c_accp{i}", [128, CQT], F32))
                for i in range(2)]
        rb_sb = s.enter_context(nc.sbuf_tensor("c_rb", [128, CQT], F32))
        dl1 = s.enter_context(nc.sbuf_tensor("c_dl1", [1, CQT], F32))
        dl2 = s.enter_context(nc.sbuf_tensor("c_dl2", [1, CQT], F32))

        sp = [s.enter_context(nc.psum_tensor(f"c_sp{i}", [128, CQT], F32))
              for i in range(4)]
        ao_p = [s.enter_context(nc.psum_tensor(f"c_aop{i}", [128, CQT], F32))
                for i in range(2)]
        den_p = s.enter_context(nc.psum_tensor("c_denp", [33, CQT], F32))
        bc_p = s.enter_context(nc.psum_tensor("c_bcp", [128, CQT], F32))

        ql = [Sem(nc, s, "c_ql0"), Sem(nc, s, "c_ql1")]
        kl = [Sem(nc, s, "c_kl0"), Sem(nc, s, "c_kl1")]
        vl = [Sem(nc, s, "c_vl0"), Sem(nc, s, "c_vl1")]
        psS = Sem(nc, s, "c_ps")
        ex = Sem(nc, s, "c_ex")
        pv = Sem(nc, s, "c_pv")
        dda = Sem(nc, s, "c_dda")
        pda = Sem(nc, s, "c_pda")
        dtr = Sem(nc, s, "c_dtr")
        dnm = Sem(nc, s, "c_dnm")
        lnd = Sem(nc, s, "c_lnd")
        rcd = Sem(nc, s, "c_rcd")
        bm = Sem(nc, s, "c_bm")
        bcc = Sem(nc, s, "c_bcc")
        nv = Sem(nc, s, "c_nv")
        nva = Sem(nc, s, "c_nva")
        st = [Sem(nc, s, "c_st0"), Sem(nc, s, "c_st1")]
        p.cc = Sem(nc, es, "c_cc")
        p.pc2 = Sem(nc, es, "c_pc2")
        p.wcast = Sem(nc, es, "c_wcast")

        items = []
        for half in range(2):
            for h in range(NHO):
                for qt in range(4):
                    items.append((h, (4 - 4 * half) + qt, half * NHO + h))
        NIDX = len(items)
        NSEND = NHO * 4

        def store_dst(h, qt):
            if qt < 4:
                return p.ao_keep[h, :, CQT * qt:CQT * (qt + 1)]
            return p.ao_send[h, :, CQT * (qt - 4):CQT * (qt - 3)]

        def act_tail(scl, idx):
            par = idx % 2
            scl.wait_ge(dnm.h, idx + 1)
            lnd.inc(scl.activation(dl1[:, :],
                                   den_p[32 * par:32 * par + 1, :], AF.Ln))

        def act_tail2(scl, idx):
            if idx >= 1:
                scl.wait_ge(bm.h, idx)
            scl.wait_ge(lnd.h, idx + 1)
            rcd.inc(scl.activation(dl2[:, :], dl1[:, :], AF.Exp, scale=-1.0))

        def act_tail3(scl, idx):
            scl.wait_ge(bm.h, idx + 1)
            if idx >= 1:
                scl.wait_ge(nv.h, idx)
            bcc.inc(scl.copy(rb_sb[:, :], bc_p[:, :]))

        with nc.Block() as blk:
            @blk.sync
            def _(sy):
                def emit_store(idx):
                    h2, qt2, _ = items[idx]
                    sy.wait_ge(nv.h, idx + 1)
                    st[idx % 2].inc(sy.dma_start(out=store_dst(h2, qt2),
                                                 in_=ao_sb[idx % 2][:, :]), 16)

                done = 0
                for li in range(2 * NHO):
                    h = items[li * 4][0]
                    par = li % 2
                    if li >= 2:
                        sy.wait_ge(pv.h, (li - 1) * 4 * NSC)
                    ql[par].inc(sy.dma_start(
                        out=q_sb[par][:, :].rearrange("p (n t) -> p n t", n=NPAN),
                        in_=p.qf[:, h, :, :].rearrange("n p t -> p n t")), 16)
                    kl[par].inc(sy.dma_start(
                        out=k_sb[par][:, :].rearrange("p (n t) -> p n t", n=NPAN),
                        in_=p.kf[:, h, :, :].rearrange("n p t -> p n t")), 16)
                    for sc in range(NSC):
                        pn, u = sc // 4, sc % 4
                        vl[par].inc(sy.dma_start(
                            out=v_sb[par][:, sc * 128:(sc + 1) * 128],
                            in_=p.vf[pn, h, :, u * 128:(u + 1) * 128],
                            transpose=True), 16)
                    if li >= 1:
                        while done < (li - 1) * 4 + 4:
                            emit_store(done)
                            done += 1
                while done < NIDX:
                    emit_store(done)
                    done += 1

            @blk.tensor
            def _(t):
                for idx, (h, qt, li) in enumerate(items):
                    par = li % 2
                    base = idx * NSC
                    apar = idx % 2
                    if qt % 4 == 0 or idx == 0:
                        t.wait_ge(ql[par].h, 16 * (li // 2 + 1))
                        t.wait_ge(kl[par].h, 16 * (li // 2 + 1))
                        t.wait_ge(vl[par].h, 16 * NSC * (li // 2 + 1))

                    def s_mm(sc):
                        psS.inc(t.matmul(
                            sp[sc % 4][:, :],
                            k_sb[par][:, sc * 128:(sc + 1) * 128],
                            q_sb[par][:, qt * CQT:(qt + 1) * CQT],
                            start=True, stop=True))

                    if idx >= 1:
                        t.wait_ge(ex.h, base - 2)
                    if idx >= 2:
                        t.wait_ge(nv.h, idx - 1)
                    s_mm(0)
                    s_mm(1)
                    if idx >= 1:
                        t.wait_ge(ex.h, base)
                    s_mm(2)
                    s_mm(3)
                    for sc in range(0, NSC, 2):
                        t.wait_ge(ex.h, base + sc + 2)
                        if sc + 4 < NSC:
                            s_mm(sc + 4)
                        if sc + 5 < NSC:
                            s_mm(sc + 5)
                        if sc == 4 and idx >= 1:
                            t.wait_ge(dtr.h, idx)
                            if idx >= 3:
                                t.wait_ge(lnd.h, idx - 2)
                            dnm.inc(t.matmul(
                                den_p[32 * ((idx - 1) % 2):
                                      32 * ((idx - 1) % 2) + 1, :],
                                p.ones128_f[:, :],
                                accd[(idx - 1) % 2][:, :],
                                start=True, stop=True))
                        if sc == 10 and idx >= 1:
                            t.wait_ge(rcd.h, idx)
                            if idx >= 2:
                                t.wait_ge(bcc.h, idx - 1)
                            bm.inc(t.matmul(bc_p[:, :], p.ones1_f[:, :],
                                            dl2[:, :], start=True, stop=True))
                        for sd in (sc, sc + 1):
                            eb = e_sb[sd % 8]
                            pv.inc(t.matmul(
                                ao_p[apar][:, :],
                                v_sb[par][:, sd * 128:(sd + 1) * 128],
                                eb[:, :],
                                start=(sd == 0), stop=(sd == NSC - 1),
                                skip_group_check=True))
                t.wait_ge(dtr.h, NIDX)
                dnm.inc(t.matmul(
                    den_p[32 * ((NIDX - 1) % 2):32 * ((NIDX - 1) % 2) + 1, :],
                    p.ones128_f[:, :],
                    accd[(NIDX - 1) % 2][:, :], start=True, stop=True))
                t.wait_ge(rcd.h, NIDX)
                t.wait_ge(bcc.h, NIDX - 1)
                bm.inc(t.matmul(bc_p[:, :], p.ones1_f[:, :],
                                dl2[:, :], start=True, stop=True))

            @blk.scalar
            def _(scl):
                for idx in range(NIDX):
                    base = idx * NSC
                    for sc in range(0, NSC, 2):
                        if sc == 6 and idx >= 1:
                            act_tail(scl, idx - 1)
                        if sc == 8 and idx >= 1:
                            act_tail2(scl, idx - 1)
                        if sc == 12 and idx >= 1:
                            act_tail3(scl, idx - 1)
                        scl.wait_ge(psS.h, base + sc + 2)
                        if base + sc >= 8:
                            scl.wait_ge(pv.h, base + sc - 6)
                            scl.wait_ge(dda.h, (base + sc) // 2 - 3)
                            scl.wait_ge(pda.h, (base + sc) // 2 - 3)
                        for sd in (sc, sc + 1):
                            ex.inc(scl.activation(e_sb[sd % 8][:, :],
                                                  sp[sd % 4][:, :], AF.Exp,
                                                  scale=RSQRT_DH))
                act_tail(scl, NIDX - 1)
                act_tail2(scl, NIDX - 1)
                act_tail3(scl, NIDX - 1)

            @blk.vector
            def _(v):
                vsq = dtr
                for idx in range(NIDX):
                    base = idx * NSC
                    apar = idx % 2
                    for sc in range(1, NSC, 2):
                        v.wait_ge(ex.h, base + sc + 1)
                        if sc > 1:
                            v.wait_ge(dda.h, dda.n)
                        if sc == 1:
                            dda.inc(v.tensor_copy(accd[apar][:, :],
                                                  e_sb[1][:, :]))
                        else:
                            dda.inc(v.tensor_tensor(accd[apar][:, :],
                                                    accd[apar][:, :],
                                                    e_sb[sc % 8][:, :],
                                                    ALU.add))
                        if sc == 15 and idx >= 1:
                            v.wait_ge(bcc.h, idx)
                            if idx >= 2:
                                v.wait_ge(st[(idx - 1) % 2].h,
                                          16 * ((idx - 1) // 2))
                            nv.inc(v.tensor_tensor(ao_sb[(idx - 1) % 2][:, :],
                                                   ao_p[(idx - 1) % 2][:, :],
                                                   rb_sb[:, :], ALU.mult))
                    v.wait_ge(pda.h, (idx + 1) * 16)
                    v.wait_ge(dda.h, dda.n)
                    dtr.inc(v.tensor_tensor(accd[apar][:, :], accd[apar][:, :],
                                            accp[apar][:, :], ALU.add))
                v.wait_ge(bcc.h, NIDX)
                v.wait_ge(st[(NIDX - 1) % 2].h, 16 * ((NIDX - 1) // 2))
                nv.inc(v.tensor_tensor(ao_sb[(NIDX - 1) % 2][:, :],
                                       ao_p[(NIDX - 1) % 2][:, :],
                                       rb_sb[:, :], ALU.mult))

            @blk.gpsimd
            def _(gp):
                casts = []
                for j in range(16):
                    rs = slice(j * 256, (j + 1) * 256)
                    casts.append((p.wo_bf[rs, :], p.woT[rs, :]))
                    casts.append((p.bw_bf[rs, :], p.bwT[rs, :]))
                for j in range(4):
                    rs = slice(j * 256, (j + 1) * 256)
                    casts.append((p.bt_bf[rs, :], p.btT[rs, :]))
                ci = 0
                for idx, (h, qt, li) in enumerate(items):
                    base = idx * NSC
                    apar = idx % 2
                    for sc in range(0, NSC, 2):
                        gp.wait_ge(ex.h, base + sc + 1)
                        if sc > 0:
                            gp.wait_ge(pda.h, pda.n)
                        if sc == 0:
                            if idx >= 2:
                                gp.wait_ge(dtr.h, idx - 1)
                            pda.inc(gp.tensor_copy(accp[apar][:, :],
                                                   e_sb[0][:, :]))
                        else:
                            pda.inc(gp.tensor_tensor(accp[apar][:, :],
                                                     accp[apar][:, :],
                                                     e_sb[sc % 8][:, :],
                                                     ALU.add))
                    if ci < len(casts):
                        o_, i_ = casts[ci]
                        p.wcast.inc(gp.dma_start(out=o_, in_=i_), 16)
                        ci += 1
                    if idx == NSEND + 7:
                        gp.wait_ge(st[0].h, 16 * ((NSEND + 1) // 2))
                        gp.wait_ge(st[1].h, 16 * (NSEND // 2))
                        p.cc.inc(gp.collective_compute(
                            "AllGather", ALU.bypass,
                            replica_groups=[[0, 1], [2, 3], [4, 5], [6, 7]],
                            ins=[p.ao_send.opt()],
                            outs=[p.ao_gath.opt()]))
                gp.wait_ge(p.cc.h, 1)
                gp.wait_ge(p.a_sem.h, 160)
                with gp.register("gra") as ra, gp.register("grb") as rb:
                    gp.reg_load(ra, p.gs_sb[0:1, 0:1])
                    gp.reg_load(rb, p.gs_sb[0:1, 1:2])
                    va = gp.snap(ra, min_val=0, max_val=1)
                    vb = gp.snap(rb, min_val=0, max_val=1)
                    gp.dma_start(out=p.ao_part.opt(),
                                 in_=p.ao_gath[1, :, :, :].opt(),
                                 cond=va < 1).then_inc(p.pc2.h, 16)
                    gp.dma_start(out=p.ao_part.opt(),
                                 in_=p.ao_gath[0, :, :, :].opt(),
                                 cond=vb < 1).then_inc(p.pc2.h, 16)
                p.pc2.n = 16
                gp.wait_ge(st[0].h, st[0].n)
                gp.wait_ge(st[1].h, st[1].n)

        p.final_waits.append((st[0], st[0].n))
        p.final_waits.append((st[1], st[1].n))



def emit_de(p):
    nc = p.nc
    es = p.es
    L = P()

    L.xmid = es.enter_context(nc.sbuf_tensor("d_xmid", [128, NOC * TQ], F32))
    L.x1b = es.enter_context(nc.sbuf_tensor("d_x1b", [128, NOC * TQ], BF16))
    L.sqb = [es.enter_context(nc.sbuf_tensor(f"d_sq{i}", [128, TQ], BF16))
             for i in range(2)]
    L.xmb = [es.enter_context(nc.sbuf_tensor(f"d_xmb{i}", [128, TQ], BF16))
             for i in range(2)]
    L.tmp = [es.enter_context(nc.sbuf_tensor(f"d_tmp{i}", [128, TQ], F32))
             for i in range(2)]
    L.mrb = es.enter_context(nc.sbuf_tensor("d_mrb", [128, 2 * TQ], F32))
    L.stat = es.enter_context(nc.sbuf_tensor("d_stat", [1, 4 * TQ], F32))
    L.rh_rep = es.enter_context(nc.sbuf_tensor("e_rhr", [128, TQ], F32))
    L.lg_sb = es.enter_context(nc.sbuf_tensor("e_lg", [16, TQ], F32))
    L.lgt = es.enter_context(nc.sbuf_tensor("e_lgt", [128, 64], F32))
    L.r16a = es.enter_context(nc.sbuf_tensor("e_r16a", [128, 64], F32))
    L.r16b = es.enter_context(nc.sbuf_tensor("e_r16b", [128, 64], F32))
    L.cbt = es.enter_context(nc.sbuf_tensor("e_cbt", [128, 64], F32))
    L.c1 = es.enter_context(nc.sbuf_tensor("e_c1", [128, 4], F32))
    L.c2 = es.enter_context(nc.sbuf_tensor("e_c2", [128, 4], F32))
    L.c3 = es.enter_context(nc.sbuf_tensor("e_c3", [128, 4], F32))
    L.cb_f = es.enter_context(nc.sbuf_tensor("e_cbf", [16, TQ], BF16))
    L.st128 = es.enter_context(nc.sbuf_tensor("e_st128", [128, 8], F32))

    L.aol = Sem(nc, es, "d_aol")
    L.xql = [Sem(nc, es, "d_xql0"), Sem(nc, es, "d_xql1")]
    L.wol = [Sem(nc, es, "d_wol0"), Sem(nc, es, "d_wol1")]
    L.pe_d = Sem(nc, es, "d_pe")
    L.vadd = Sem(nc, es, "d_vadd")
    L.vc = Sem(nc, es, "d_vc")
    L.sqa = Sem(nc, es, "d_sqa")
    L.pst = Sem(nc, es, "d_pst")
    L.vstat = Sem(nc, es, "d_vstat")
    L.astat = Sem(nc, es, "d_astat")
    L.bmd = Sem(nc, es, "d_bmd")
    L.mrc = Sem(nc, es, "d_mrc")
    L.dap = Sem(nc, es, "d_dap")
    L.x1v = Sem(nc, es, "d_x1v")
    L.artl = Sem(nc, es, "e_artl")
    L.bwl = [Sem(nc, es, "e_bwl0"), Sem(nc, es, "e_bwl1")]
    L.btl = [Sem(nc, es, "e_btl0"), Sem(nc, es, "e_btl1")]
    L.arp = Sem(nc, es, "e_arp")
    L.arv = Sem(nc, es, "e_arv")
    L.ltp = Sem(nc, es, "e_ltp")
    L.lte = Sem(nc, es, "e_lte")
    L.vm0 = Sem(nc, es, "e_vm0")
    L.aex = Sem(nc, es, "e_aex")
    L.vch = Sem(nc, es, "e_vch")
    L.tch = Sem(nc, es, "e_tch")
    L.ctp = Sem(nc, es, "e_ctp")
    L.cbf_s = Sem(nc, es, "e_cbf")
    L.gbm = Sem(nc, es, "e_gbm")
    L.gv = Sem(nc, es, "e_gv")
    L.fpe = Sem(nc, es, "e_fpe")
    L.x2v = Sem(nc, es, "e_x2v")
    L.tps = Sem(nc, es, "e_tps")
    L.xtc = Sem(nc, es, "e_xtc")
    L.l2s = Sem(nc, es, "e_l2s")
    L.l2var = Sem(nc, es, "e_l2var")
    L.l2a = Sem(nc, es, "e_l2a")
    L.l2v = Sem(nc, es, "e_l2v")
    L.vch2 = Sem(nc, es, "e_vch2")
    L.outst = Sem(nc, es, "e_out")

    for tq in range(NQUAD):
        emit_d_quarter(p, tq, L)
        emit_e_quarter(p, tq, L)
        nc.all_engine_barrier()

    p.final_waits.append((L.outst, L.outst.n))


def emit_d_quarter(p, tq, L):
    nc = p.nc
    with contextlib.ExitStack() as s:
        L.ao_a = s.enter_context(nc.sbuf_tensor(f"d_ao{tq}", [128, NOC * TQ], BF16))
        L.xq = [s.enter_context(nc.sbuf_tensor(f"d_xq{tq}_{i}", [128, TQ], BF16))
                for i in range(2)]
        L.wo_w = [s.enter_context(nc.sbuf_tensor(f"d_wo{tq}_{i}",
                                                 [128, NIC * 128], BF16))
                  for i in range(2)]
        xat = [s.enter_context(nc.psum_tensor(f"d_xat{tq}_{i}", [128, TQ], F32))
               for i in range(2)]
        sum_p = s.enter_context(nc.psum_tensor(f"d_sum{tq}", [1, TQ], F32))
        sq_p = s.enter_context(nc.psum_tensor(f"d_sqp{tq}", [1, TQ], F32))

        with nc.Block() as blk:
            @blk.sync
            def _(sy):
                sy.wait_ge(p.pc2.h, 16)
                for fc in range(NOC):
                    src = (p.ao_keep[fc, :, tq * TQ:(tq + 1) * TQ] if fc < NHO
                           else p.ao_part[fc - NHO, :, tq * TQ:(tq + 1) * TQ])
                    L.aol.inc(sy.dma_start(out=L.ao_a[:, fc * TQ:(fc + 1) * TQ],
                                           in_=src), 16)
                for oc in range(NOC):
                    if oc >= 2:
                        sy.wait_ge(L.vadd.h, tq * NOC + oc - 1)
                    L.xql[oc % 2].inc(sy.dma_start(
                        out=L.xq[oc % 2][:, :],
                        in_=p.xf[oc, :, tq * TQ:(tq + 1) * TQ]), 16)

            @blk.gpsimd
            def _(g):
                if tq == 0:
                    g.wait_ge(p.wcast.h, p.wcast.n)
                for oc in range(NOC):
                    gx = tq * NOC + oc
                    if oc >= 2:
                        g.wait_ge(L.pe_d.h, gx - 1)
                    wsrc = p.wo_bf[:, oc * 128:(oc + 1) * 128]
                    wsrc = wsrc.rearrange("(c p) o -> p c o", p=128)
                    wdst = L.wo_w[oc % 2][:, :].rearrange("q (c o) -> q c o",
                                                          c=NIC)
                    L.wol[oc % 2].inc(g.dma_start(out=wdst, in_=wsrc), 16)

            @blk.tensor
            def _(t):
                t.wait_ge(L.aol.h, 16 * NOC * (tq + 1))
                for oc in range(NOC):
                    gx = tq * NOC + oc
                    t.wait_ge(L.wol[oc % 2].h, 16 * (tq * 16 + oc // 2 + 1))
                    if oc >= 2:
                        t.wait_ge(L.vadd.h, gx - 1)
                    for ic in range(NIC):
                        mm = t.matmul(xat[oc % 2][:, :],
                                      L.wo_w[oc % 2][:, ic * 128:(ic + 1) * 128],
                                      L.ao_a[:, ic * TQ:(ic + 1) * TQ],
                                      start=(ic == 0), stop=(ic == NIC - 1))
                    L.pe_d.inc(mm)
                    socs = [oc - 1] if oc >= 1 else []
                    if oc == NOC - 1:
                        socs.append(oc)
                    for soc in socs:
                        t.wait_ge(L.vc.h, tq * NOC + soc + 1)
                        t.wait_ge(L.sqa.h, tq * NOC + soc + 1)
                        t.matmul(sum_p[:, :], p.ones_bf[:, :],
                                 L.xmb[soc % 2][:, :],
                                 start=(soc == 0), stop=(soc == NOC - 1),
                                 skip_group_check=True)
                        L.pst.inc(t.matmul(sq_p[:, :], p.ones_bf[:, :],
                                           L.sqb[soc % 2][:, :],
                                           start=(soc == 0), stop=(soc == NOC - 1),
                                           skip_group_check=True))
                t.wait_ge(L.astat.h, 2 * (tq + 1))
                t.matmul(xat[0][:, :], p.ones1_f[:, :],
                         L.stat[:, 0:TQ], start=True, stop=True)
                L.bmd.inc(t.matmul(xat[1][:, :], p.ones1_f[:, :],
                                   L.stat[:, 2 * TQ:3 * TQ],
                                   start=True, stop=True))

            @blk.vector
            def _(v):
                for oc in range(NOC):
                    gx = tq * NOC + oc
                    v.wait_ge(L.pe_d.h, gx + 1)
                    v.wait_ge(L.xql[oc % 2].h, 16 * (tq * 16 + oc // 2 + 1))
                    L.vadd.inc(v.tensor_tensor(L.xmid[:, oc * TQ:(oc + 1) * TQ],
                                               xat[oc % 2][:, :],
                                               L.xq[oc % 2][:, :],
                                               ALU.add))
                    v.wait_ge(L.vadd.h, gx + 1)
                    if oc >= 2:
                        v.wait_ge(L.pst.h, gx - 1)
                    L.vc.inc(v.tensor_copy(L.xmb[oc % 2][:, :],
                                           L.xmid[:, oc * TQ:(oc + 1) * TQ]))
                v.wait_ge(L.pst.h, NOC * (tq + 1))
                L.vstat.inc(v.tensor_scalar(L.stat[:, 0:TQ], sum_p[:, :],
                                            1.0 / D, None, ALU.mult))
                L.vstat.wait(v)
                L.vstat.inc(v.tensor_scalar(L.stat[:, TQ:2 * TQ], sq_p[:, :],
                                            1.0 / D, None, ALU.mult))
                L.vstat.wait(v)
                L.vstat.inc(v.tensor_tensor(L.stat[:, 3 * TQ:4 * TQ],
                                            L.stat[:, 0:TQ], L.stat[:, 0:TQ],
                                            ALU.mult))
                L.vstat.wait(v)
                L.vstat.inc(v.tensor_tensor(L.stat[:, TQ:2 * TQ],
                                            L.stat[:, TQ:2 * TQ],
                                            L.stat[:, 3 * TQ:4 * TQ],
                                            ALU.subtract))
                v.wait_ge(L.mrc.h, 2 * (tq + 1))
                for oc in range(NOC):
                    par = oc % 2
                    L.dap.inc(v.tensor_tensor(L.tmp[par][:, :],
                                              L.xmid[:, oc * TQ:(oc + 1) * TQ],
                                              L.mrb[:, 0:TQ], ALU.subtract))
                    L.dap.wait(v)
                    L.x1v.inc(v.tensor_tensor(L.x1b[:, oc * TQ:(oc + 1) * TQ],
                                              L.tmp[par][:, :],
                                              L.mrb[:, TQ:2 * TQ], ALU.mult))

            @blk.scalar
            def _(scl):
                for oc in range(NOC):
                    gx = tq * NOC + oc
                    scl.wait_ge(L.vadd.h, gx + 1)
                    if oc >= 2:
                        scl.wait_ge(L.pst.h, gx - 1)
                    L.sqa.inc(scl.activation(L.sqb[oc % 2][:, :],
                                             L.xmid[:, oc * TQ:(oc + 1) * TQ],
                                             AF.Square))
                scl.wait_ge(L.vstat.h, 4 * (tq + 1))
                L.astat.inc(scl.activation(L.stat[:, 3 * TQ:4 * TQ],
                                           L.stat[:, TQ:2 * TQ], AF.Ln,
                                           bias=p.eps_sb[0:1, 0:1]))
                L.astat.wait(scl)
                L.astat.inc(scl.activation(L.stat[:, 2 * TQ:3 * TQ],
                                           L.stat[:, 3 * TQ:4 * TQ], AF.Exp,
                                           scale=-0.5))
                scl.wait_ge(L.bmd.h, tq + 1)
                L.mrc.inc(scl.copy(L.mrb[:, 0:TQ], xat[0][:, :]))
                L.mrc.inc(scl.copy(L.mrb[:, TQ:2 * TQ], xat[1][:, :]))


def emit_e_quarter(p, tq, L):
    nc = p.nc
    with contextlib.ExitStack() as s:
        L.bw_w = [s.enter_context(nc.sbuf_tensor(f"e_bw{tq}_{i}",
                                                 [128, NIC * 128], BF16))
                  for i in range(2)]
        L.bt_w = [s.enter_context(nc.sbuf_tensor(f"e_bt{tq}_{i}",
                                                 [128, ERC * 128], BF16))
                  for i in range(2)]
        L.g_sb = s.enter_context(nc.sbuf_tensor(f"e_g{tq}", [128, ERC * TQ], BF16))
        L.art_sb = s.enter_context(nc.sbuf_tensor(f"e_art{tq}", [128, NIC * 80],
                                                  BF16))
        L.xt_sb = s.enter_context(nc.sbuf_tensor(f"e_xt{tq}", [128, D], F32))
        L.sqt = s.enter_context(nc.sbuf_tensor(f"e_sqt{tq}", [128, D], BF16))
        L.out_sb = s.enter_context(nc.sbuf_tensor(f"e_out{tq}", [128, D], F32))
        ar_p = s.enter_context(nc.psum_tensor(f"e_ar{tq}", [80, TQ], F32))
        ffn_p = [s.enter_context(nc.psum_tensor(f"e_ffn{tq}_{i}", [128, TQ], F32))
                 for i in range(2)]
        gb_p = [s.enter_context(nc.psum_tensor(f"e_gb{tq}_{i}", [128, TQ], F32))
                for i in range(2)]
        tp_p = [s.enter_context(nc.psum_tensor(f"e_tp{tq}_{i}", [128, 128], F32))
                for i in range(2)]
        lt_p = tp_p[0]
        ct_p = tp_p[1]

        with nc.Block() as blk:
            @blk.gpsimd
            def _(g):
                g.wait_ge(L.pe_d.h, NOC * (tq + 1))
                g.wait_ge(L.vadd.h, NOC * (tq + 1))
                asrc = p.art.rearrange("(c p) o -> p c o", p=128)
                adst = L.art_sb[:, :].rearrange("q (c o) -> q c o", c=NIC)
                L.artl.inc(g.dma_start(out=adst, in_=asrc), 16)
                for oc in range(NOC):
                    gx = tq * NOC + oc
                    if oc >= 2:
                        g.wait_ge(L.fpe.h, gx - 1)
                    wsrc = p.bw_bf[:, oc * 128:(oc + 1) * 128]
                    wsrc = wsrc.rearrange("(c p) o -> p c o", p=128)
                    wdst = L.bw_w[oc % 2][:, :].rearrange("q (c o) -> q c o",
                                                          c=NIC)
                    L.bwl[oc % 2].inc(g.dma_start(out=wdst, in_=wsrc), 16)
                    bsrc = p.bt_bf[:, oc * 128:(oc + 1) * 128]
                    bsrc = bsrc.rearrange("(c p) o -> p c o", p=128)
                    bdst = L.bt_w[oc % 2][:, :].rearrange("q (c o) -> q c o",
                                                          c=ERC)
                    L.btl[oc % 2].inc(g.dma_start(out=bdst, in_=bsrc), 16)
                for tb in range(4):
                    k = 4 * tq + tb
                    g.wait_ge(L.l2v.h, k + 1)
                    L.outst.inc(g.dma_start(
                        out=p.out[tq * TQ + tb * 128:tq * TQ + (tb + 1) * 128, :],
                        in_=L.out_sb[:, :]), 16)
                g.wait_ge(L.outst.h, L.outst.n)

            @blk.tensor
            def _(t):
                if tq == 0:
                    t.wait_ge(p.a_sem.h, 16 * 9)
                t.wait_ge(L.artl.h, 16 * (tq + 1))
                for ic in range(NIC):
                    mm = t.matmul(ar_p[:, :],
                                  L.art_sb[:, ic * 80:(ic + 1) * 80],
                                  L.x1b[:, ic * TQ:(ic + 1) * TQ],
                                  start=(ic == 0), stop=(ic == NIC - 1))
                L.arp.inc(mm)
                t.wait_ge(L.arv.h, 2 * (tq + 1))
                for tb in range(4):
                    if tb > 0:
                        t.wait_ge(L.lte.h, 4 * tq + tb)
                    L.ltp.inc(t.transpose(lt_p[:, 0:16],
                                          L.lg_sb[:, tb * 128:(tb + 1) * 128],
                                          p.id_f[0:16, 0:16]))
                t.wait_ge(L.tch.h, tq + 1)
                for tb in range(4):
                    if tb > 0:
                        t.wait_ge(L.cbf_s.h, 4 * tq + tb)
                    L.ctp.inc(t.transpose(ct_p[0:16, :],
                                          L.cbt[:, tb * 16:(tb + 1) * 16],
                                          p.id_f[:, :]))
                t.wait_ge(L.cbf_s.h, 4 * (tq + 1))
                for c in range(ERC):
                    if c >= 2:
                        t.wait_ge(L.gv.h, tq * ERC + c - 1)
                    L.gbm.inc(t.matmul(gb_p[c % 2][:, :],
                                       p.sel[:, c * 128:(c + 1) * 128],
                                       cb_rhs(L), start=True, stop=True))
                t.wait_ge(L.gv.h, ERC * (tq + 1))
                for oc in range(NOC):
                    gx = tq * NOC + oc
                    t.wait_ge(L.bwl[oc % 2].h, 16 * (tq * 16 + oc // 2 + 1))
                    t.wait_ge(L.btl[oc % 2].h, 16 * (tq * 16 + oc // 2 + 1))
                    if oc >= 2:
                        t.wait_ge(L.x2v.h, gx - 1)
                    for ic in range(NIC):
                        t.matmul(ffn_p[oc % 2][:, :],
                                 L.bw_w[oc % 2][:, ic * 128:(ic + 1) * 128],
                                 L.x1b[:, ic * TQ:(ic + 1) * TQ],
                                 start=(ic == 0), stop=False,
                                 skip_group_check=True)
                    for c in range(ERC):
                        t.matmul(ffn_p[oc % 2][:, :],
                                 L.bt_w[oc % 2][:, c * 128:(c + 1) * 128],
                                 L.g_sb[:, c * TQ:(c + 1) * TQ],
                                 start=False, stop=False,
                                 skip_group_check=True)
                    L.fpe.inc(t.matmul(ffn_p[oc % 2][:, :],
                                       p.bb_bf[0:1, oc * 128:(oc + 1) * 128],
                                       p.ones512_bf[0:1, :],
                                       start=False, stop=True,
                                       skip_group_check=True))
                for tb in range(4):
                    for oc in range(NOC):
                        k = tb * NOC + oc
                        t.wait_ge(L.x2v.h, tq * NOC + oc + 1)
                        if k >= 2:
                            t.wait_ge(L.xtc.h, tq * 128 + k - 1)
                        L.tps.inc(t.transpose(
                            tp_p[k % 2][:, :],
                            L.xmid[:, oc * TQ + tb * 128:oc * TQ + (tb + 1) * 128],
                            p.id_f[:, :]))

            @blk.scalar
            def _(scl):
                scl.wait_ge(L.arp.h, tq + 1)
                scl.copy(L.rh_rep[0:64, :], ar_p[0:64, :])
                L.arv.inc(scl.copy(L.rh_rep[64:128, :], ar_p[0:64, :]))
                L.arv.inc(scl.copy(L.lg_sb[:, :], ar_p[64:80, :]))
                scl.wait_ge(L.vm0.h, tq + 1)
                for tb in range(4):
                    ai = scl.activation(L.r16a[:, tb * 16:(tb + 1) * 16],
                                        L.lgt[:, tb * 16:(tb + 1) * 16],
                                        AF.Exp, bias=L.c2[:, tb:tb + 1])
                L.aex.inc(ai)
                for tb in range(4):
                    scl.wait_ge(L.ctp.h, 4 * tq + tb + 1)
                    L.cbf_s.inc(scl.copy(L.cb_f[:, tb * 128:(tb + 1) * 128],
                                         ct_p[0:16, :]))
                for tb in range(4):
                    k = 4 * tq + tb
                    for oc in range(NOC):
                        kk = tb * NOC + oc
                        scl.wait_ge(L.tps.h, tq * 128 + kk + 1)
                        if oc == 0:
                            scl.wait_ge(L.l2v.h, k)
                        L.xtc.inc(scl.copy(L.xt_sb[:, oc * 128:(oc + 1) * 128],
                                           tp_p[kk % 2][:, :]))
                    scl.wait_ge(L.l2s.h, k)
                    L.l2a.inc(scl.activation(L.sqt[:, :], L.xt_sb[:, :],
                                             AF.Square))
                    scl.wait_ge(L.l2var.h, k + 1)
                    L.l2a.inc(scl.activation(L.st128[:, 6:7], L.st128[:, 5:6],
                                             AF.Ln, bias=p.eps_sb[:, 0:1]))
                    L.l2a.wait(scl)
                    L.l2a.inc(scl.activation(L.st128[:, 7:8], L.st128[:, 6:7],
                                             AF.Exp, scale=-0.5))

            @blk.vector
            def _(v):
                for tb in range(4):
                    v.wait_ge(L.ltp.h, 4 * tq + tb + 1)
                    if tq > 0 and tb == 0:
                        v.wait_ge(L.ctp.h, 4 * tq)
                    L.lte.inc(v.tensor_copy(L.lgt[:, tb * 16:(tb + 1) * 16],
                                            lt_p[:, 0:16]))
                v.wait_ge(L.lte.h, 4 * (tq + 1))
                for tb in range(4):
                    cv = v.reduce_max(L.c1[:, tb:tb + 1],
                                      L.lgt[:, tb * 16:(tb + 1) * 16], AX.X)
                L.vch.inc(cv)
                L.vch.wait(v)
                L.vm0.inc(v.tensor_scalar(L.c2[:, 0:4], L.c1[:, 0:4], -1.0,
                                          None, ALU.mult))
                v.wait_ge(L.aex.h, tq + 1)

                def tiles(buf):
                    return [buf[:, tb * 16:(tb + 1) * 16] for tb in range(4)]

                ra, rb_, ct = tiles(L.r16a), tiles(L.r16b), tiles(L.cbt)
                for tb in range(4):
                    v.reduce_sum(L.c1[:, tb:tb + 1], ra[tb], AX.X)
                for tb in range(4):
                    cv = v.reduce_max(L.c2[:, tb:tb + 1], ra[tb], AX.X)
                L.vch.inc(cv)
                L.vch.wait(v)
                for tb in range(4):
                    cv = v.tensor_scalar(rb_[tb], ra[tb], L.c2[:, tb:tb + 1],
                                         -1e9, ALU.is_ge, ALU.mult)
                L.vch.inc(cv)
                L.vch.wait(v)
                for tb in range(4):
                    cv = v.tensor_tensor(rb_[tb], ra[tb], rb_[tb], ALU.add)
                L.vch.inc(cv)
                L.vch.wait(v)
                for tb in range(4):
                    cv = v.reduce_max(L.c3[:, tb:tb + 1], rb_[tb], AX.X)
                L.vch.inc(cv)
                L.vch.wait(v)
                for tb in range(4):
                    v.tensor_scalar(rb_[tb], ra[tb], L.c3[:, tb:tb + 1],
                                    None, ALU.is_ge)
                L.vch.inc(v.tensor_scalar(L.c1[:, 0:4], L.c1[:, 0:4], 1e-6,
                                          None, ALU.mult))
                L.vch.wait(v)
                L.vch.inc(v.tensor_tensor(L.c2[:, 0:4], L.c2[:, 0:4],
                                          L.c3[:, 0:4], ALU.add))
                L.vch.wait(v)
                L.vch.inc(v.tensor_tensor(L.c1[:, 0:4], L.c1[:, 0:4],
                                          L.c2[:, 0:4], ALU.add))
                L.vch.wait(v)
                L.vch.inc(v.reciprocal(L.c2[:, 0:4], L.c1[:, 0:4]))
                L.vch.wait(v)
                for tb in range(4):
                    cv = v.tensor_mul(ct[tb], ra[tb], rb_[tb])
                L.vch.inc(cv)
                L.vch.wait(v)
                for tb in range(4):
                    cv = v.tensor_scalar(ct[tb], ct[tb], L.c2[:, tb:tb + 1],
                                         None, ALU.mult)
                L.tch.inc(cv)
                for c in range(ERC):
                    v.wait_ge(L.gbm.h, tq * ERC + c + 1)
                    L.gv.inc(v.tensor_tensor(L.g_sb[:, c * TQ:(c + 1) * TQ],
                                             gb_p[c % 2][:, :],
                                             L.rh_rep[:, :], ALU.mult))
                for oc in range(NOC):
                    gx = tq * NOC + oc
                    v.wait_ge(L.fpe.h, gx + 1)
                    L.x2v.inc(v.tensor_tensor(L.xmid[:, oc * TQ:(oc + 1) * TQ],
                                              ffn_p[oc % 2][:, :],
                                              L.x1b[:, oc * TQ:(oc + 1) * TQ],
                                              ALU.add))
                for tb in range(4):
                    k = 4 * tq + tb
                    v.wait_ge(L.xtc.h, tq * 128 + (tb + 1) * NOC)
                    L.vch2.inc(v.reduce_sum(L.st128[:, 0:1], L.xt_sb[:, :], AX.X))
                    L.vch2.wait(v)
                    v.wait_ge(L.l2a.h, 3 * k + 1)
                    L.l2s.inc(v.reduce_sum(L.st128[:, 1:2], L.sqt[:, :], AX.X))
                    L.l2s.wait(v)
                    L.vch2.inc(v.tensor_scalar(L.st128[:, 2:3], L.st128[:, 0:1],
                                               1.0 / D, None, ALU.mult))
                    L.vch2.wait(v)
                    L.vch2.inc(v.tensor_scalar(L.st128[:, 3:4], L.st128[:, 1:2],
                                               1.0 / D, None, ALU.mult))
                    L.vch2.wait(v)
                    L.vch2.inc(v.tensor_tensor(L.st128[:, 4:5], L.st128[:, 2:3],
                                               L.st128[:, 2:3], ALU.mult))
                    L.vch2.wait(v)
                    L.l2var.inc(v.tensor_tensor(L.st128[:, 5:6], L.st128[:, 3:4],
                                                L.st128[:, 4:5], ALU.subtract))
                    v.wait_ge(L.l2a.h, 3 * k + 3)
                    if k > 0:
                        v.wait_ge(L.outst.h, 16 * k)
                    L.l2v.inc(v.tensor_scalar(L.out_sb[:, :], L.xt_sb[:, :],
                                              L.st128[:, 2:3], L.st128[:, 7:8],
                                              ALU.subtract, ALU.mult))


def cb_rhs(L):
    return L.cb_f[:, :]



def emit_dumps(p):
    nc = p.nc
    es = p.es
    ds = Sem(nc, es, "dump_s")
    outs = []
    if "tables" in p.dump:
        outs += [("d_cos", p.cos[:, :], [128, S], F32),
                 ("d_sina", p.sina[:, :], [128, S], F32),
                 ("d_idf", p.id_f[:, :], [128, 128], F32),
                 ("d_sel", p.sel[:, :], [16, ERC * 128], BF16)]
    if "qkv" in p.dump:
        outs += [("d_qf", p.qf, [NPAN, NHO, 128, PAN], BF16),
                 ("d_kf", p.kf, [NPAN, NHO, 128, PAN], BF16),
                 ("d_vf", p.vf, [NPAN, NHO, 128, PAN], BF16),
                 ("d_xbf", p.xbf, [S, D], BF16)]
    if "ao" in p.dump:
        outs += [("d_aok", p.ao_keep, [NHO, 128, TOWN], BF16),
                 ("d_aop", p.ao_part, [NHO, 128, TOWN], BF16)]
    with nc.Block() as blk:
        @blk.gpsimd
        def _(g):
            if "ao" in p.dump:
                g.wait_ge(p.pc2.h, 16)
            for name, src, shape, dt in outs:
                o = nc.dram_tensor(name, shape, dt, kind="ExternalOutput").ap()
                ds.inc(g.dma_start(out=o, in_=src), 16)
    p.final_waits.append((ds, ds.n))


def emit_final_wait(p):
    nc = p.nc
    with nc.Block() as blk:
        @blk.gpsimd
        def _(g):
            for sem, n in p.final_waits:
                g.wait_ge(sem.h, n)



def make_in_maps(inputs):
    x = np.asarray(inputs["L0_attn_hidden_states"])
    wq = np.asarray(inputs["L0_attn_Wq"])
    wk = np.asarray(inputs["L0_attn_Wk"])
    wv = np.asarray(inputs["L0_attn_Wv"])
    wo = np.asarray(inputs["L0_attn_Wo"])
    bw = np.asarray(inputs["L0_ffn_base_w"])
    bbias = np.asarray(inputs["L0_ffn_base_b"])
    A = np.asarray(inputs["L0_ffn_A"])
    B = np.asarray(inputs["L0_ffn_B"])
    rw = np.asarray(inputs["L0_ffn_router_w"])

    woT = np.ascontiguousarray(wo.T)
    woT_rot = np.ascontiguousarray(np.concatenate([woT[2048:], woT[:2048]], axis=0))
    bwT = np.ascontiguousarray(bw.T)
    art = np.ascontiguousarray(np.concatenate([A.T, rw.T], axis=1))
    btT = np.ascontiguousarray(B.transpose(0, 2, 1).reshape(1024, D))
    bb = np.ascontiguousarray(bbias.reshape(1, D))
    invf_c = (10000.0 ** (-(np.arange(128) % 64) / 64.0)).astype(
        np.float32).reshape(128, 1)
    tpos0 = np.arange(S, dtype=np.float32).reshape(1, S)
    tpos1 = np.ascontiguousarray(
        np.concatenate([np.arange(2048, S), np.arange(0, 2048)])
        .astype(np.float32).reshape(1, S))

    in_maps = []
    for c in range(8):
        b, g = c // 2, c % 2
        sl = slice(2048 * g, 2048 * (g + 1))
        xbc = x[b] if g == 0 else np.ascontiguousarray(
            np.concatenate([x[b, 2048:], x[b, :2048]], axis=0))
        in_maps.append({
            "xb": np.ascontiguousarray(xbc),
            "tpos": tpos0 if g == 0 else tpos1,
            "gsel": np.array([[g, 1 - g]], dtype=np.int32),
            "invf": invf_c,
            "wqT": np.ascontiguousarray(wq.T[:, sl]),
            "wkT": np.ascontiguousarray(wk.T[:, sl]),
            "wvT": np.ascontiguousarray(wv.T[:, sl]),
            "woT": woT if g == 0 else woT_rot,
            "bwT": bwT, "bb": bb, "art": art, "btT": btT,
        })
    return in_maps


def kernel(**inputs):
    in_maps = make_in_maps(inputs)
    nc = build()
    res = run_bass_kernel_spmd(nc, in_maps, core_ids=list(range(8)))
    out = np.empty((4, S, D), np.float32)
    for c in range(8):
        b, g = c // 2, c % 2
        out[b, 2048 * g:2048 * (g + 1), :] = res.results[c]["out"]
    return out


# version 15
# speedup vs baseline: 1.0321x; 1.0321x over previous
import contextlib
import math

import numpy as np

import concourse.bass as bass
import concourse.mybir as mybir
from concourse.bass_utils import run_bass_kernel_spmd

F32 = mybir.dt.float32
BF16 = mybir.dt.bfloat16
I32 = mybir.dt.int32
AF = mybir.ActivationFunctionType
ALU = mybir.AluOpType
AX = mybir.AxisListType

D = 4096
S = 4096
TOWN = 2048
NHO = 16
PAN = 512
NPAN = S // PAN
QT = 1024
NQT = S // QT
NSC = S // 128
NIC = D // 128
NOC = D // 128
NQUAD = 4
TQ = TOWN // NQUAD
ERC = 8
TWO_PI = 2.0 * math.pi
RSQRT_DH = 1.0 / math.sqrt(128.0)
LOG_BASE = math.log(10000.0)
EPS = 1e-5
import numpy as _np
CW1 = float(_np.float32(6.28125))
CW2 = float(_np.float32(2.0 * math.pi - 6.28125))
CW3 = float(2.0 * math.pi - float(_np.float32(6.28125)) -
            float(_np.float32(2.0 * math.pi - 6.28125)))


class Sem:
    def __init__(self, nc, es, name):
        self.h = es.enter_context(nc.semaphore(name))
        self.n = 0

    def inc(self, instr, k=1):
        instr.then_inc(self.h, k)
        self.n += k
        return self.n

    def wait(self, eng, n=None):
        eng.wait_ge(self.h, self.n if n is None else n)


class P:
    pass


def build(phases=("init", "a", "b", "c", "de"), dump=()):
    nc = bass.Bass("TRN2", target_bir_lowering=False, debug=False)
    p = P()
    p.nc = nc
    p.dump = set(dump)
    p.final_waits = []

    p.xb = nc.dram_tensor("xb", [S, D], F32, kind="ExternalInput").ap()
    p.tpos = nc.dram_tensor("tpos", [1, S], F32, kind="ExternalInput").ap()
    p.gsel = nc.dram_tensor("gsel", [1, 2], I32, kind="ExternalInput").ap()
    p.invf = nc.dram_tensor("invf", [128, 1], F32, kind="ExternalInput").ap()
    p.wqT = nc.dram_tensor("wqT", [D, 2048], F32, kind="ExternalInput").ap()
    p.wkT = nc.dram_tensor("wkT", [D, 2048], F32, kind="ExternalInput").ap()
    p.wvT = nc.dram_tensor("wvT", [D, 2048], F32, kind="ExternalInput").ap()
    p.woT = nc.dram_tensor("woT", [D, D], F32, kind="ExternalInput").ap()
    p.bwT = nc.dram_tensor("bwT", [D, D], F32, kind="ExternalInput").ap()
    p.bb = nc.dram_tensor("bb", [1, D], F32, kind="ExternalInput").ap()
    p.art = nc.dram_tensor("art", [D, 80], F32, kind="ExternalInput").ap()
    p.btT = nc.dram_tensor("btT", [1024, D], F32, kind="ExternalInput").ap()

    p.out = nc.dram_tensor("out", [TOWN, D], F32, kind="ExternalOutput").ap()

    p.xbf = nc.dram_tensor("xbf", [S, D], BF16, kind="Internal").ap()
    p.xf = nc.dram_tensor("xf", [NIC, 128, S], BF16, kind="Internal").ap()
    p.qf = nc.dram_tensor("qf", [NPAN, NHO, 128, PAN], BF16, kind="Internal").ap()
    p.kf = nc.dram_tensor("kf", [NPAN, NHO, 128, PAN], BF16, kind="Internal").ap()
    p.vf = nc.dram_tensor("vf", [NPAN, NHO, 128, PAN], BF16, kind="Internal").ap()
    p.ao_keep = nc.dram_tensor("ao_keep", [NHO, 128, TOWN], BF16, kind="Internal").ap()
    p.ao_send = nc.dram_tensor("ao_send", [NHO, 128, TOWN], BF16, kind="Internal").ap()
    p.ao_gath = nc.dram_tensor("ao_gath", [2, NHO, 128, TOWN], BF16, kind="Internal").ap()
    p.ao_part = nc.dram_tensor("ao_part", [NHO, 128, TOWN], BF16, kind="Internal").ap()
    p.wo_bf = nc.dram_tensor("wo_bf", [D, D], BF16, kind="Internal").ap()
    p.bw_bf = nc.dram_tensor("bw_bf", [D, D], BF16, kind="Internal").ap()
    p.bt_bf = nc.dram_tensor("bt_bf", [1024, D], BF16, kind="Internal").ap()

    with contextlib.ExitStack() as es:
        p.es = es
        p.ones_bf = es.enter_context(nc.sbuf_tensor("ones_bf", [128, 1], BF16))
        p.ones1_f = es.enter_context(nc.sbuf_tensor("ones1_f", [1, 128], F32))
        p.ones512_bf = es.enter_context(nc.sbuf_tensor("ones512_bf", [1, 512], BF16))
        p.id_f = es.enter_context(nc.sbuf_tensor("id_f", [128, 128], F32))
        p.id_bf = es.enter_context(nc.sbuf_tensor("id_bf", [128, 128], BF16))
        p.sel = es.enter_context(nc.sbuf_tensor("sel", [16, ERC * 128], BF16))
        p.bb_bf = es.enter_context(nc.sbuf_tensor("bb_bf", [1, D], BF16))
        p.eps_sb = es.enter_context(nc.sbuf_tensor("eps_sb", [128, 1], F32))
        p.ones128_f = es.enter_context(nc.sbuf_tensor("ones128_f", [128, 1], F32))
        p.gs_sb = es.enter_context(nc.sbuf_tensor("gs_sb", [1, 2], I32))

        p.a_sem = Sem(nc, es, "a_sem")

        with contextlib.ExitStack() as bscope:
            p.cos = bscope.enter_context(nc.sbuf_tensor("cos", [128, S], F32))
            p.sina = bscope.enter_context(nc.sbuf_tensor("sina", [128, S], F32))
            if "a" in phases:
                emit_a(p)
            if "init" in phases:
                emit_init(p)
            if "a" in phases:
                emit_a2(p)
            if "init" in phases:
                nc.all_engine_barrier()
            if "b" in phases:
                emit_b(p)
                nc.all_engine_barrier()
        if "c" in phases:
            emit_c(p)
            nc.all_engine_barrier()
        if "de" in phases:
            emit_de(p)
        if p.dump:
            nc.all_engine_barrier()
            emit_dumps(p)
        emit_final_wait(p)
    return nc



def emit_init(p):
    import os
    lvl = int(os.environ.get("MK_INIT_LVL", "3"))
    nc = p.nc
    with contextlib.ExitStack() as s:
        iof = s.enter_context(nc.sbuf_tensor("iof", [128, S], F32))
        sc1 = s.enter_context(nc.sbuf_tensor("sc1", [128, S], F32))
        sc2 = s.enter_context(nc.sbuf_tensor("sc2", [128, S], F32))
        sc3 = s.enter_context(nc.sbuf_tensor("sc3", [128, S], F32))
        pp = s.enter_context(nc.sbuf_tensor("pp", [128, 1], F32))
        invf = s.enter_context(nc.sbuf_tensor("invf_sb", [128, 1], F32))
        tps = s.enter_context(nc.sbuf_tensor("tps", [1, S], F32))
        t1 = s.enter_context(nc.sbuf_tensor("t16_1", [16, 128], F32))
        t2 = s.enter_context(nc.sbuf_tensor("t16_2", [16, 128], F32))
        tpp = s.enter_context(nc.psum_tensor("tpp", [128, QT], F32))
        gs = Sem(nc, s, "ini_g")
        vs = Sem(nc, s, "ini_v")
        ss = Sem(nc, s, "ini_s")
        ts_ = Sem(nc, s, "ini_t")
        ls = Sem(nc, s, "ini_l")

        with nc.Block() as blk:
            @blk.sync
            def _(sy):
                ls.inc(sy.dma_start(out=tps[:, :], in_=p.tpos[:, :]), 16)
                ls.inc(sy.dma_start(out=invf[:, :], in_=p.invf[:, :]), 16)

            @blk.gpsimd
            def _(g):
                g.iota(iof[:, :], [[1, S]], channel_multiplier=0,
                       allow_small_or_imprecise_dtypes=True)
                gs.inc(g.iota(pp[:, :], [[0, 1]], channel_multiplier=1,
                              allow_small_or_imprecise_dtypes=True))

            @blk.vector
            def _(v):
                v.memset(p.ones_bf[:, :], 1.0)
                v.memset(p.eps_sb[:, :], EPS)
                v.memset(p.ones128_f[:, :], 1.0)
                v.memset(p.ones1_f[:, :], 1.0)
                v.memset(p.ones512_bf[:, :], 1.0)
                gs.wait(v)
                vs.inc(v.tensor_scalar(p.id_f[:, :], iof[:, 0:128], pp[:, 0:1],
                                       None, ALU.is_equal))
                vs.wait(v)
                vs.inc(v.tensor_copy(p.id_bf[:, :], p.id_f[:, :]))
                vs.inc(v.tensor_scalar(t1[:, :], iof[0:16, 0:128], 64.0, None,
                                       ALU.is_ge))
                vs.wait(v)
                for c in range(ERC):
                    vs.inc(v.tensor_scalar_add(t2[:, :], t1[:, :], float(2 * c)))
                    vs.wait(v)
                    vs.inc(v.tensor_scalar(p.sel[:, c * 128:(c + 1) * 128],
                                           t2[:, :], pp[0:16, 0:1], 0.5,
                                           ALU.is_equal, ALU.mult))
                    vs.wait(v)
                pass

            @blk.scalar
            def _(scl):
                vs.wait(scl)
                scl.wait_ge(ls.h, 32)
                ss.inc(scl.copy(sc3[:, 0:1], invf[:, :]))

        if lvl < 2:
            return
        NJ = int(os.environ.get("MK_NJ", str(S // QT)))
        with nc.Block() as blk:
            @blk.tensor
            def _(t):
                ls.wait(t, 32)
                vs.wait(t)
                for j in range(NJ):
                    if j >= 1:
                        t.wait_ge(ss.h, 1 + j)
                    t.matmul(tpp[:, 0:512], p.ones1_f[:, :],
                             tps[:, j * QT:j * QT + 512], start=True, stop=True)
                    ts_.inc(t.matmul(tpp[:, 512:QT], p.ones1_f[:, :],
                                     tps[:, j * QT + 512:(j + 1) * QT],
                                     start=True, stop=True))

            @blk.scalar
            def _(scl):
                for j in range(NJ):
                    scl.wait_ge(ts_.h, j + 1)
                    ss.inc(scl.copy(iof[:, j * QT:(j + 1) * QT], tpp[:, :]))

        def tables():
            H = [slice(0, S // 2), slice(S // 2, S)]
            with nc.Block() as blk:
                @blk.vector
                def _(v):
                    ss.wait(v)
                    for hs in H:
                        vs.inc(v.tensor_scalar(sc1[:, hs], iof[:, hs],
                                               invf[:, 0:1], None, ALU.mult))
                    vs.wait(v)
                    for hs in H:
                        vs.inc(v.tensor_copy(p.sina[:, hs], sc1[:, hs]))
                    v.memset(sc2[:, :], 0.0)
                    vs.wait(v)
                    for j in range(9, -1, -1):
                        bj = TWO_PI * (1 << j)
                        for hs in H:
                            vs.inc(v.tensor_scalar(sc3[:, hs], p.sina[:, hs],
                                                   bj, None, ALU.is_ge))
                        vs.wait(v)
                        for hs in H:
                            vs.inc(v.scalar_tensor_tensor(
                                p.sina[:, hs], sc3[:, hs], -bj, p.sina[:, hs],
                                ALU.mult, ALU.add))
                            vs.inc(v.scalar_tensor_tensor(
                                sc2[:, hs], sc3[:, hs], float(1 << j),
                                sc2[:, hs], ALU.mult, ALU.add))
                        vs.wait(v)
                    for cw in (CW1, CW2, CW3):
                        for hs in H:
                            vs.inc(v.scalar_tensor_tensor(
                                sc1[:, hs], sc2[:, hs], -cw, sc1[:, hs],
                                ALU.mult, ALU.add))
                        vs.wait(v)
                    for bnd, per in ((-math.pi, TWO_PI), (math.pi, -TWO_PI)):
                        op = ALU.is_lt if per > 0 else ALU.is_ge
                        for hs in H:
                            vs.inc(v.tensor_scalar(sc3[:, hs], sc1[:, hs],
                                                   bnd, None, op))
                        vs.wait(v)
                        for hs in H:
                            vs.inc(v.scalar_tensor_tensor(
                                sc1[:, hs], sc3[:, hs], per, sc1[:, hs],
                                ALU.mult, ALU.add))
                        vs.wait(v)
                    for hs in H:
                        vs.inc(v.tensor_scalar_add(sc2[:, hs], sc1[:, hs],
                                                   math.pi / 2.0))
                    vs.wait(v)
                    for hs in H:
                        vs.inc(v.tensor_scalar(sc3[:, hs], sc2[:, hs],
                                               math.pi, None, ALU.is_ge))
                    vs.wait(v)
                    for hs in H:
                        vs.inc(v.scalar_tensor_tensor(
                            sc2[:, hs], sc3[:, hs], -TWO_PI, sc2[:, hs],
                            ALU.mult, ALU.add))

                @blk.scalar
                def _(scl):
                    vs.wait(scl)
                    ss.inc(scl.activation(p.sina[:, :], sc1[:, :], AF.Sin))
                    ss.inc(scl.activation(p.cos[:, :], sc2[:, :], AF.Sin))

        tables()

        with nc.Block() as blk:
            @blk.vector
            def _(v):
                ss.wait(v)
                vs.inc(v.tensor_scalar(p.sina[0:64, :], p.sina[0:64, :], -1.0,
                                       None, ALU.mult))



def emit_a(p):
    nc = p.nc
    with nc.Block() as blk:
        @blk.gpsimd
        def _(g):
            for j in range(NPAN):
                p.a_sem.inc(g.dma_start(out=p.xbf[:, j * PAN:(j + 1) * PAN],
                                        in_=p.xb[:, j * PAN:(j + 1) * PAN]), 16)
            p.a_sem.inc(g.dma_start(out=p.bb_bf[:, :], in_=p.bb[:, :]), 16)
            p.a_sem.inc(g.dma_start(out=p.gs_sb[:, :], in_=p.gsel[:, :]), 16)


def emit_a2(p):
    nc = p.nc
    es = p.es
    p.xfs = [Sem(nc, es, "a2_xfs0"), Sem(nc, es, "a2_xfs1")]
    with contextlib.ExitStack() as s:
        stg = [s.enter_context(nc.sbuf_tensor(f"a2_st{i}", [128, S], BF16))
               for i in range(2)]
        tl = [Sem(nc, s, "a2_tl0"), Sem(nc, s, "a2_tl1")]
        with nc.Block() as blk:
            @blk.sync
            def _(sy):
                for ic in range(NIC):
                    par = ic % 2
                    sy.wait_ge(p.a_sem.h, 16 * (ic // 4 + 1))
                    if ic >= 2:
                        sy.wait_ge(p.xfs[par].h, 16 * (ic // 2))
                    tl[par].inc(sy.dma_start(
                        out=stg[par][:, :],
                        in_=p.xbf[:, ic * 128:(ic + 1) * 128],
                        transpose=True), 16)

            @blk.gpsimd
            def _(g):
                for ic in range(NIC):
                    par = ic % 2
                    g.wait_ge(tl[par].h, 16 * (ic // 2 + 1))
                    p.xfs[par].inc(g.dma_start(out=p.xf[ic, :, :],
                                               in_=stg[par][:, :]), 16)



def emit_b(p):
    nc = p.nc

    CHW = 4
    PW = CHW * 128

    with contextlib.ExitStack() as s:
        b_w = [s.enter_context(nc.sbuf_tensor(f"b_w{i}", [128, CHW * D], BF16))
               for i in range(2)]
        b_pan = [s.enter_context(nc.sbuf_tensor(f"b_pan{i}", [128, NIC * PAN],
                                                BF16))
                 for i in range(2)]
        b_stg = [s.enter_context(nc.sbuf_tensor(f"b_stg{i}", [128, CHW * PAN],
                                                BF16))
                 for i in range(2)]
        b_qc = [s.enter_context(nc.sbuf_tensor(f"b_qc{i}", [128, PAN], F32))
                for i in range(2)]
        b_qs = [s.enter_context(nc.sbuf_tensor(f"b_qs{i}", [128, PAN], F32))
                for i in range(2)]
        b_t1 = [s.enter_context(nc.sbuf_tensor(f"b_t1{i}", [128, PAN], F32))
                for i in range(2)]
        b_t2 = [s.enter_context(nc.sbuf_tensor(f"b_t2{i}", [128, PAN], F32))
                for i in range(2)]
        b_ps = [s.enter_context(nc.psum_tensor(f"b_ps{i}", [128, PAN], F32))
                for i in range(2)]

        wl = [Sem(nc, s, "b_wl0"), Sem(nc, s, "b_wl1")]
        pl = [Sem(nc, s, "b_pl0"), Sem(nc, s, "b_pl1")]
        pe = Sem(nc, s, "b_pe")
        ae = Sem(nc, s, "b_ae")
        dt_ = Sem(nc, s, "b_dt")
        ck = Sem(nc, s, "b_ck")
        ckv = Sem(nc, s, "b_ckv")
        st = [Sem(nc, s, "b_sts0"), Sem(nc, s, "b_sts1")]

        passes = []
        for kind, src, dst in (("q", p.wqT, p.qf), ("k", p.wkT, p.kf),
                               ("v", p.wvT, p.vf)):
            for quarter in range(4):
                passes.append((kind, src, dst, quarter))
        NP = len(passes)

        chunks = []
        nae = nck = nckv = 0
        for pi, (kind, src, dst, quarter) in enumerate(passes):
            rope = kind != "v"
            for pn in range(NPAN):
                for oc in range(CHW):
                    c = {"pi": pi, "pn": pn, "oc": oc, "gi": pi * NPAN + pn,
                         "rope": rope, "pe_after": len(chunks) + 1}
                    if rope:
                        nae += 1
                        nck += 1
                    else:
                        nckv += 1
                    c["ae_after"] = nae
                    c["ck_after"] = nck
                    c["ckv_after"] = nckv
                    chunks.append(c)

        def bank_free_wait(eng, ci):
            if ci < 2:
                return
            c2 = chunks[ci - 2]
            if c2["rope"]:
                eng.wait_ge(ae.h, c2["ae_after"])
            else:
                eng.wait_ge(ckv.h, c2["ckv_after"])

        with nc.Block() as blk:
            @blk.gpsimd
            def _(g):
                g.wait_ge(p.xfs[0].h, p.xfs[0].n)
                g.wait_ge(p.xfs[1].h, p.xfs[1].n)
                for pi, (kind, src, dst, quarter) in enumerate(passes):
                    if pi > 1:
                        g.wait_ge(pe.h, (pi - 1) * NPAN * CHW)
                    for oc in range(CHW):
                        wsrc = src[:, quarter * PW + oc * 128:
                                   quarter * PW + (oc + 1) * 128]
                        wsrc = wsrc.rearrange("(c p) o -> p c o", p=128)
                        wdst = b_w[pi % 2][:, oc * D:(oc + 1) * D].rearrange(
                            "q (c o) -> q c o", c=NIC)
                        wl[pi % 2].inc(g.dma_start(out=wdst, in_=wsrc), 16)
                    for pn in range(NPAN):
                        gi = pi * NPAN + pn
                        if kind != "v":
                            g.wait_ge(ck.h, (gi + 1) * CHW)
                        else:
                            g.wait_ge(ckv.h, (gi - 8 * NPAN + 1) * CHW)
                        odst = dst[pn, quarter * CHW:(quarter + 1) * CHW, :, :]
                        st[gi % 2].inc(g.dma_start(
                            out=odst.rearrange("n p t -> p n t"),
                            in_=b_stg[gi % 2][:, :].rearrange(
                                "p (n t) -> p n t", n=CHW)), 16)
                g.wait_ge(st[0].h, st[0].n)
                g.wait_ge(st[1].h, st[1].n)

            @blk.sync
            def _(sy):
                for pi in range(NP):
                    for pn in range(NPAN):
                        gi = pi * NPAN + pn
                        par = gi % 2
                        if gi >= 2:
                            sy.wait_ge(pe.h, (gi - 1) * CHW)
                        if pi == 0 and pn == 0:
                            sy.wait_ge(p.xfs[0].h, p.xfs[0].n)
                            sy.wait_ge(p.xfs[1].h, p.xfs[1].n)
                        for j in range(4):
                            xsrc = p.xf[j * 8:(j + 1) * 8, :,
                                        pn * PAN:(pn + 1) * PAN]
                            pl[par].inc(sy.dma_start(
                                out=b_pan[par][:, j * 8 * PAN:(j + 1) * 8 * PAN]
                                    .rearrange("q (c t) -> q c t", c=8),
                                in_=xsrc.rearrange("c p t -> p c t")), 16)

            @blk.tensor
            def _(t):
                ci = 0
                for pi, (kind, src, dst, quarter) in enumerate(passes):
                    t.wait_ge(wl[pi % 2].h, (pi // 2 + 1) * CHW * 16)
                    for pn in range(NPAN):
                        gi = pi * NPAN + pn
                        t.wait_ge(pl[gi % 2].h, (gi // 2 + 1) * 4 * 16)
                        buf = b_pan[gi % 2]
                        for oc in range(CHW):
                            bank_free_wait(t, ci)
                            for ic in range(NIC):
                                mm = t.matmul(
                                    b_ps[ci % 2][:, :],
                                    b_w[pi % 2][:, oc * D + ic * 128:
                                                oc * D + (ic + 1) * 128],
                                    buf[:, ic * PAN:(ic + 1) * PAN],
                                    start=(ic == 0), stop=(ic == NIC - 1))
                            pe.inc(mm)
                            ci += 1

            @blk.scalar
            def _(scl):
                for ci, c in enumerate(chunks):
                    scl.wait_ge(pe.h, c["pe_after"])
                    if c["rope"]:
                        if ci >= 2 and chunks[ci - 2]["rope"]:
                            scl.wait_ge(dt_.h, chunks[ci - 2]["ae_after"])
                        scl.copy(b_qc[ci % 2][:, :], b_ps[ci % 2][:, :])
                        scl.copy(b_qs[ci % 2][0:64, :], b_ps[ci % 2][64:128, :])
                        ae.inc(scl.copy(b_qs[ci % 2][64:128, :],
                                        b_ps[ci % 2][0:64, :]))
                    else:
                        if c["gi"] >= 2 and c["oc"] == 0:
                            scl.wait_ge(st[c["gi"] % 2].h, (c["gi"] // 2) * 16)
                        ckv.inc(scl.copy(
                            b_stg[c["gi"] % 2][:, c["oc"] * PAN:(c["oc"] + 1) * PAN],
                            b_ps[ci % 2][:, :]))

            @blk.vector
            def _(v):
                for ci, c in enumerate(chunks):
                    if not c["rope"]:
                        continue
                    par = ci % 2
                    gi, pn, oc = c["gi"], c["pn"], c["oc"]
                    tsl = slice(pn * PAN, (pn + 1) * PAN)
                    v.wait_ge(ae.h, c["ae_after"])
                    if ci >= 2 and chunks[ci - 2]["rope"]:
                        v.wait_ge(ck.h, chunks[ci - 2]["ck_after"])
                    qc = b_qc[par]
                    v.tensor_tensor(b_t1[par][:, :], qc[:, :],
                                    p.cos[:, tsl], ALU.mult)
                    dt_.inc(v.tensor_tensor(b_t2[par][:, :], b_qs[par][:, :],
                                            p.sina[:, tsl], ALU.mult))
                    v.wait_ge(dt_.h, c["ae_after"])
                    if gi >= 2 and oc == 0:
                        v.wait_ge(st[gi % 2].h, (gi // 2) * 16)
                    ck.inc(v.tensor_tensor(
                        b_stg[gi % 2][:, oc * PAN:(oc + 1) * PAN],
                        b_t1[par][:, :], b_t2[par][:, :], ALU.add))

        p.final_waits.append((st[0], st[0].n))
        p.final_waits.append((st[1], st[1].n))



CQT = 512
CNQT = S // CQT


def emit_c(p):
    nc = p.nc
    es = p.es

    with contextlib.ExitStack() as s:
        q_sb = [s.enter_context(nc.sbuf_tensor(f"c_q{i}", [128, S], BF16))
                for i in range(2)]
        k_sb = [s.enter_context(nc.sbuf_tensor(f"c_k{i}", [128, S], BF16))
                for i in range(2)]
        v_sb = [s.enter_context(nc.sbuf_tensor(f"c_v{i}", [128, S], BF16))
                for i in range(2)]
        e_sb = [s.enter_context(nc.sbuf_tensor(f"c_e{i}", [128, CQT], BF16))
                for i in range(8)]
        ao_sb = [s.enter_context(nc.sbuf_tensor(f"c_ao{i}", [128, CQT], BF16))
                 for i in range(2)]
        accd = [s.enter_context(nc.sbuf_tensor(f"c_accd{i}", [128, CQT], F32))
                for i in range(2)]
        accp = [s.enter_context(nc.sbuf_tensor(f"c_accp{i}", [128, CQT], F32))
                for i in range(2)]
        rb_sb = s.enter_context(nc.sbuf_tensor("c_rb", [128, CQT], F32))
        dl1 = s.enter_context(nc.sbuf_tensor("c_dl1", [1, CQT], F32))
        dl2 = s.enter_context(nc.sbuf_tensor("c_dl2", [1, CQT], F32))

        sp = [s.enter_context(nc.psum_tensor(f"c_sp{i}", [128, CQT], F32))
              for i in range(4)]
        ao_p = [s.enter_context(nc.psum_tensor(f"c_aop{i}", [128, CQT], F32))
                for i in range(2)]
        den_p = s.enter_context(nc.psum_tensor("c_denp", [33, CQT], F32))
        bc_p = s.enter_context(nc.psum_tensor("c_bcp", [128, CQT], F32))

        ql = [Sem(nc, s, "c_ql0"), Sem(nc, s, "c_ql1")]
        kl = [Sem(nc, s, "c_kl0"), Sem(nc, s, "c_kl1")]
        vl = [Sem(nc, s, "c_vl0"), Sem(nc, s, "c_vl1")]
        psS = Sem(nc, s, "c_ps")
        ex = Sem(nc, s, "c_ex")
        pv = Sem(nc, s, "c_pv")
        dda = Sem(nc, s, "c_dda")
        pda = Sem(nc, s, "c_pda")
        dtr = Sem(nc, s, "c_dtr")
        dnm = Sem(nc, s, "c_dnm")
        lnd = Sem(nc, s, "c_lnd")
        rcd = Sem(nc, s, "c_rcd")
        bm = Sem(nc, s, "c_bm")
        bcc = Sem(nc, s, "c_bcc")
        nv = Sem(nc, s, "c_nv")
        nva = Sem(nc, s, "c_nva")
        st = [Sem(nc, s, "c_st0"), Sem(nc, s, "c_st1")]
        p.cc = Sem(nc, es, "c_cc")
        p.pc2 = Sem(nc, es, "c_pc2")
        p.wcast = Sem(nc, es, "c_wcast")

        items = []
        for half in range(2):
            for h in range(NHO):
                for qt in range(4):
                    items.append((h, (4 - 4 * half) + qt, half * NHO + h))
        NIDX = len(items)
        NSEND = NHO * 4

        def store_dst(h, qt):
            if qt < 4:
                return p.ao_keep[h, :, CQT * qt:CQT * (qt + 1)]
            return p.ao_send[h, :, CQT * (qt - 4):CQT * (qt - 3)]

        def act_tail(scl, idx):
            par = idx % 2
            scl.wait_ge(dnm.h, idx + 1)
            lnd.inc(scl.activation(dl1[:, :],
                                   den_p[32 * par:32 * par + 1, :], AF.Ln))

        def act_tail2(scl, idx):
            if idx >= 1:
                scl.wait_ge(bm.h, idx)
            scl.wait_ge(lnd.h, idx + 1)
            rcd.inc(scl.activation(dl2[:, :], dl1[:, :], AF.Exp, scale=-1.0))

        def act_tail3(scl, idx):
            scl.wait_ge(bm.h, idx + 1)
            if idx >= 1:
                scl.wait_ge(nv.h, idx)
            bcc.inc(scl.copy(rb_sb[:, :], bc_p[:, :]))

        with nc.Block() as blk:
            @blk.sync
            def _(sy):
                def emit_store(idx):
                    h2, qt2, _ = items[idx]
                    sy.wait_ge(nv.h, idx + 1)
                    st[idx % 2].inc(sy.dma_start(out=store_dst(h2, qt2),
                                                 in_=ao_sb[idx % 2][:, :]), 16)

                done = 0
                for li in range(2 * NHO):
                    h = items[li * 4][0]
                    par = li % 2
                    if li >= 2:
                        sy.wait_ge(pv.h, (li - 1) * 4 * NSC)
                    ql[par].inc(sy.dma_start(
                        out=q_sb[par][:, :].rearrange("p (n t) -> p n t", n=NPAN),
                        in_=p.qf[:, h, :, :].rearrange("n p t -> p n t")), 16)
                    kl[par].inc(sy.dma_start(
                        out=k_sb[par][:, :].rearrange("p (n t) -> p n t", n=NPAN),
                        in_=p.kf[:, h, :, :].rearrange("n p t -> p n t")), 16)
                    for sc in range(NSC):
                        pn, u = sc // 4, sc % 4
                        vl[par].inc(sy.dma_start(
                            out=v_sb[par][:, sc * 128:(sc + 1) * 128],
                            in_=p.vf[pn, h, :, u * 128:(u + 1) * 128],
                            transpose=True), 16)
                    if li >= 1:
                        while done < (li - 1) * 4 + 4:
                            emit_store(done)
                            done += 1
                while done < NIDX:
                    emit_store(done)
                    done += 1

            @blk.tensor
            def _(t):
                for idx, (h, qt, li) in enumerate(items):
                    par = li % 2
                    base = idx * NSC
                    apar = idx % 2
                    if qt % 4 == 0 or idx == 0:
                        t.wait_ge(ql[par].h, 16 * (li // 2 + 1))
                        t.wait_ge(kl[par].h, 16 * (li // 2 + 1))
                        t.wait_ge(vl[par].h, 16 * NSC * (li // 2 + 1))

                    def s_mm(sc):
                        psS.inc(t.matmul(
                            sp[sc % 4][:, :],
                            k_sb[par][:, sc * 128:(sc + 1) * 128],
                            q_sb[par][:, qt * CQT:(qt + 1) * CQT],
                            start=True, stop=True))

                    if idx >= 1:
                        t.wait_ge(ex.h, base - 2)
                    if idx >= 2:
                        t.wait_ge(nv.h, idx - 1)
                    s_mm(0)
                    s_mm(1)
                    if idx >= 1:
                        t.wait_ge(ex.h, base)
                    s_mm(2)
                    s_mm(3)
                    for sc in range(0, NSC, 2):
                        t.wait_ge(ex.h, base + sc + 2)
                        if sc + 4 < NSC:
                            s_mm(sc + 4)
                        if sc + 5 < NSC:
                            s_mm(sc + 5)
                        if sc == 4 and idx >= 1:
                            t.wait_ge(dtr.h, idx)
                            if idx >= 3:
                                t.wait_ge(lnd.h, idx - 2)
                            dnm.inc(t.matmul(
                                den_p[32 * ((idx - 1) % 2):
                                      32 * ((idx - 1) % 2) + 1, :],
                                p.ones128_f[:, :],
                                accd[(idx - 1) % 2][:, :],
                                start=True, stop=True))
                        if sc == 10 and idx >= 1:
                            t.wait_ge(rcd.h, idx)
                            if idx >= 2:
                                t.wait_ge(bcc.h, idx - 1)
                            bm.inc(t.matmul(bc_p[:, :], p.ones1_f[:, :],
                                            dl2[:, :], start=True, stop=True))
                        for sd in (sc, sc + 1):
                            eb = e_sb[sd % 8]
                            pv.inc(t.matmul(
                                ao_p[apar][:, :],
                                v_sb[par][:, sd * 128:(sd + 1) * 128],
                                eb[:, :],
                                start=(sd == 0), stop=(sd == NSC - 1),
                                skip_group_check=True))
                t.wait_ge(dtr.h, NIDX)
                dnm.inc(t.matmul(
                    den_p[32 * ((NIDX - 1) % 2):32 * ((NIDX - 1) % 2) + 1, :],
                    p.ones128_f[:, :],
                    accd[(NIDX - 1) % 2][:, :], start=True, stop=True))
                t.wait_ge(rcd.h, NIDX)
                t.wait_ge(bcc.h, NIDX - 1)
                bm.inc(t.matmul(bc_p[:, :], p.ones1_f[:, :],
                                dl2[:, :], start=True, stop=True))

            @blk.scalar
            def _(scl):
                for idx in range(NIDX):
                    base = idx * NSC
                    for sc in range(0, NSC, 2):
                        if sc == 6 and idx >= 1:
                            act_tail(scl, idx - 1)
                        if sc == 8 and idx >= 1:
                            act_tail2(scl, idx - 1)
                        if sc == 12 and idx >= 1:
                            act_tail3(scl, idx - 1)
                        scl.wait_ge(psS.h, base + sc + 2)
                        if base + sc >= 8:
                            scl.wait_ge(pv.h, base + sc - 6)
                            scl.wait_ge(dda.h, (base + sc) // 2 - 3)
                            scl.wait_ge(pda.h, (base + sc) // 2 - 3)
                        for sd in (sc, sc + 1):
                            ex.inc(scl.activation(e_sb[sd % 8][:, :],
                                                  sp[sd % 4][:, :], AF.Exp,
                                                  scale=RSQRT_DH))
                act_tail(scl, NIDX - 1)
                act_tail2(scl, NIDX - 1)
                act_tail3(scl, NIDX - 1)

            @blk.vector
            def _(v):
                vsq = dtr
                for idx in range(NIDX):
                    base = idx * NSC
                    apar = idx % 2
                    for sc in range(1, NSC, 2):
                        v.wait_ge(ex.h, base + sc + 1)
                        if sc > 1:
                            v.wait_ge(dda.h, dda.n)
                        if sc == 1:
                            dda.inc(v.tensor_copy(accd[apar][:, :],
                                                  e_sb[1][:, :]))
                        else:
                            dda.inc(v.tensor_tensor(accd[apar][:, :],
                                                    accd[apar][:, :],
                                                    e_sb[sc % 8][:, :],
                                                    ALU.add))
                        if sc == 15 and idx >= 1:
                            v.wait_ge(bcc.h, idx)
                            if idx >= 2:
                                v.wait_ge(st[(idx - 1) % 2].h,
                                          16 * ((idx - 1) // 2))
                            nv.inc(v.tensor_tensor(ao_sb[(idx - 1) % 2][:, :],
                                                   ao_p[(idx - 1) % 2][:, :],
                                                   rb_sb[:, :], ALU.mult))
                    v.wait_ge(pda.h, (idx + 1) * 16)
                    v.wait_ge(dda.h, dda.n)
                    dtr.inc(v.tensor_tensor(accd[apar][:, :], accd[apar][:, :],
                                            accp[apar][:, :], ALU.add))
                v.wait_ge(bcc.h, NIDX)
                v.wait_ge(st[(NIDX - 1) % 2].h, 16 * ((NIDX - 1) // 2))
                nv.inc(v.tensor_tensor(ao_sb[(NIDX - 1) % 2][:, :],
                                       ao_p[(NIDX - 1) % 2][:, :],
                                       rb_sb[:, :], ALU.mult))

            @blk.gpsimd
            def _(gp):
                casts = []
                for j in range(16):
                    rs = slice(j * 256, (j + 1) * 256)
                    casts.append((p.wo_bf[rs, :], p.woT[rs, :]))
                    casts.append((p.bw_bf[rs, :], p.bwT[rs, :]))
                for j in range(4):
                    rs = slice(j * 256, (j + 1) * 256)
                    casts.append((p.bt_bf[rs, :], p.btT[rs, :]))
                ci = 0
                for idx, (h, qt, li) in enumerate(items):
                    base = idx * NSC
                    apar = idx % 2
                    for sc in range(0, NSC, 2):
                        gp.wait_ge(ex.h, base + sc + 1)
                        if sc > 0:
                            gp.wait_ge(pda.h, pda.n)
                        if sc == 0:
                            if idx >= 2:
                                gp.wait_ge(dtr.h, idx - 1)
                            pda.inc(gp.tensor_copy(accp[apar][:, :],
                                                   e_sb[0][:, :]))
                        else:
                            pda.inc(gp.tensor_tensor(accp[apar][:, :],
                                                     accp[apar][:, :],
                                                     e_sb[sc % 8][:, :],
                                                     ALU.add))
                    if ci < len(casts):
                        o_, i_ = casts[ci]
                        p.wcast.inc(gp.dma_start(out=o_, in_=i_), 16)
                        ci += 1
                    if idx == NSEND + 7:
                        gp.wait_ge(st[0].h, 16 * ((NSEND + 1) // 2))
                        gp.wait_ge(st[1].h, 16 * (NSEND // 2))
                        p.cc.inc(gp.collective_compute(
                            "AllGather", ALU.bypass,
                            replica_groups=[[0, 1], [2, 3], [4, 5], [6, 7]],
                            ins=[p.ao_send.opt()],
                            outs=[p.ao_gath.opt()]))
                gp.wait_ge(p.cc.h, 1)
                gp.wait_ge(p.a_sem.h, 160)
                with gp.register("gra") as ra, gp.register("grb") as rb:
                    gp.reg_load(ra, p.gs_sb[0:1, 0:1])
                    gp.reg_load(rb, p.gs_sb[0:1, 1:2])
                    va = gp.snap(ra, min_val=0, max_val=1)
                    vb = gp.snap(rb, min_val=0, max_val=1)
                    gp.dma_start(out=p.ao_part.opt(),
                                 in_=p.ao_gath[1, :, :, :].opt(),
                                 cond=va < 1).then_inc(p.pc2.h, 16)
                    gp.dma_start(out=p.ao_part.opt(),
                                 in_=p.ao_gath[0, :, :, :].opt(),
                                 cond=vb < 1).then_inc(p.pc2.h, 16)
                p.pc2.n = 16
                gp.wait_ge(st[0].h, st[0].n)
                gp.wait_ge(st[1].h, st[1].n)

        p.final_waits.append((st[0], st[0].n))
        p.final_waits.append((st[1], st[1].n))



def emit_de(p):
    nc = p.nc
    es = p.es
    L = P()

    L.xmid = es.enter_context(nc.sbuf_tensor("d_xmid", [128, NOC * TQ], F32))
    L.x1b = es.enter_context(nc.sbuf_tensor("d_x1b", [128, NOC * TQ], BF16))
    L.sqb = [es.enter_context(nc.sbuf_tensor(f"d_sq{i}", [128, TQ], BF16))
             for i in range(2)]
    L.xmb = [es.enter_context(nc.sbuf_tensor(f"d_xmb{i}", [128, TQ], BF16))
             for i in range(2)]
    L.tmp = [es.enter_context(nc.sbuf_tensor(f"d_tmp{i}", [128, TQ], F32))
             for i in range(2)]
    L.mrb = es.enter_context(nc.sbuf_tensor("d_mrb", [128, 2 * TQ], F32))
    L.stat = es.enter_context(nc.sbuf_tensor("d_stat", [1, 4 * TQ], F32))
    L.rh_rep = es.enter_context(nc.sbuf_tensor("e_rhr", [128, TQ], F32))
    L.lg_sb = es.enter_context(nc.sbuf_tensor("e_lg", [16, TQ], F32))
    L.lgt = es.enter_context(nc.sbuf_tensor("e_lgt", [128, 64], F32))
    L.r16a = es.enter_context(nc.sbuf_tensor("e_r16a", [128, 64], F32))
    L.r16b = es.enter_context(nc.sbuf_tensor("e_r16b", [128, 64], F32))
    L.cbt = es.enter_context(nc.sbuf_tensor("e_cbt", [128, 64], F32))
    L.c1 = es.enter_context(nc.sbuf_tensor("e_c1", [128, 4], F32))
    L.c2 = es.enter_context(nc.sbuf_tensor("e_c2", [128, 4], F32))
    L.c3 = es.enter_context(nc.sbuf_tensor("e_c3", [128, 4], F32))
    L.cb_f = es.enter_context(nc.sbuf_tensor("e_cbf", [16, TQ], BF16))
    L.st128 = es.enter_context(nc.sbuf_tensor("e_st128", [128, 8], F32))

    L.aol = Sem(nc, es, "d_aol")
    L.xql = [Sem(nc, es, "d_xql0"), Sem(nc, es, "d_xql1")]
    L.wol = [Sem(nc, es, "d_wol0"), Sem(nc, es, "d_wol1")]
    L.pe_d = Sem(nc, es, "d_pe")
    L.vadd = Sem(nc, es, "d_vadd")
    L.vc = Sem(nc, es, "d_vc")
    L.sqa = Sem(nc, es, "d_sqa")
    L.pst = Sem(nc, es, "d_pst")
    L.vstat = Sem(nc, es, "d_vstat")
    L.astat = Sem(nc, es, "d_astat")
    L.bmd = Sem(nc, es, "d_bmd")
    L.mrc = Sem(nc, es, "d_mrc")
    L.dap = Sem(nc, es, "d_dap")
    L.x1v = Sem(nc, es, "d_x1v")
    L.artl = Sem(nc, es, "e_artl")
    L.bwl = [Sem(nc, es, "e_bwl0"), Sem(nc, es, "e_bwl1")]
    L.btl = [Sem(nc, es, "e_btl0"), Sem(nc, es, "e_btl1")]
    L.arp = Sem(nc, es, "e_arp")
    L.arv = Sem(nc, es, "e_arv")
    L.ltp = Sem(nc, es, "e_ltp")
    L.lte = Sem(nc, es, "e_lte")
    L.vm0 = Sem(nc, es, "e_vm0")
    L.aex = Sem(nc, es, "e_aex")
    L.vch = Sem(nc, es, "e_vch")
    L.tch = Sem(nc, es, "e_tch")
    L.ctp = Sem(nc, es, "e_ctp")
    L.cbf_s = Sem(nc, es, "e_cbf")
    L.gbm = Sem(nc, es, "e_gbm")
    L.gv = Sem(nc, es, "e_gv")
    L.fpe = Sem(nc, es, "e_fpe")
    L.x2v = Sem(nc, es, "e_x2v")
    L.tps = Sem(nc, es, "e_tps")
    L.xtc = Sem(nc, es, "e_xtc")
    L.l2s = Sem(nc, es, "e_l2s")
    L.l2var = Sem(nc, es, "e_l2var")
    L.l2a = Sem(nc, es, "e_l2a")
    L.l2v = Sem(nc, es, "e_l2v")
    L.vch2 = Sem(nc, es, "e_vch2")
    L.outst = Sem(nc, es, "e_out")

    for tq in range(NQUAD):
        emit_d_quarter(p, tq, L)
        emit_e_quarter(p, tq, L)
        nc.all_engine_barrier()

    p.final_waits.append((L.outst, L.outst.n))


def emit_d_quarter(p, tq, L):
    nc = p.nc
    with contextlib.ExitStack() as s:
        L.ao_a = s.enter_context(nc.sbuf_tensor(f"d_ao{tq}", [128, NOC * TQ], BF16))
        L.xq = [s.enter_context(nc.sbuf_tensor(f"d_xq{tq}_{i}", [128, TQ], BF16))
                for i in range(2)]
        L.wo_w = [s.enter_context(nc.sbuf_tensor(f"d_wo{tq}_{i}",
                                                 [128, NIC * 128], BF16))
                  for i in range(2)]
        xat = [s.enter_context(nc.psum_tensor(f"d_xat{tq}_{i}", [128, TQ], F32))
               for i in range(2)]
        sum_p = s.enter_context(nc.psum_tensor(f"d_sum{tq}", [1, TQ], F32))
        sq_p = s.enter_context(nc.psum_tensor(f"d_sqp{tq}", [1, TQ], F32))

        with nc.Block() as blk:
            @blk.sync
            def _(sy):
                sy.wait_ge(p.pc2.h, 16)
                for fc in range(NOC):
                    src = (p.ao_keep[fc, :, tq * TQ:(tq + 1) * TQ] if fc < NHO
                           else p.ao_part[fc - NHO, :, tq * TQ:(tq + 1) * TQ])
                    L.aol.inc(sy.dma_start(out=L.ao_a[:, fc * TQ:(fc + 1) * TQ],
                                           in_=src), 16)
                for oc in range(NOC):
                    if oc >= 2:
                        sy.wait_ge(L.vadd.h, tq * NOC + oc - 1)
                    L.xql[oc % 2].inc(sy.dma_start(
                        out=L.xq[oc % 2][:, :],
                        in_=p.xf[oc, :, tq * TQ:(tq + 1) * TQ]), 16)

            @blk.gpsimd
            def _(g):
                if tq == 0:
                    g.wait_ge(p.wcast.h, p.wcast.n)
                for oc in range(NOC):
                    gx = tq * NOC + oc
                    if oc >= 2:
                        g.wait_ge(L.pe_d.h, gx - 1)
                    wsrc = p.wo_bf[:, oc * 128:(oc + 1) * 128]
                    wsrc = wsrc.rearrange("(c p) o -> p c o", p=128)
                    wdst = L.wo_w[oc % 2][:, :].rearrange("q (c o) -> q c o",
                                                          c=NIC)
                    L.wol[oc % 2].inc(g.dma_start(out=wdst, in_=wsrc), 16)

            @blk.tensor
            def _(t):
                t.wait_ge(L.aol.h, 16 * NOC * (tq + 1))
                for oc in range(NOC):
                    gx = tq * NOC + oc
                    t.wait_ge(L.wol[oc % 2].h, 16 * (tq * 16 + oc // 2 + 1))
                    if oc >= 2:
                        t.wait_ge(L.vadd.h, gx - 1)
                    for ic in range(NIC):
                        mm = t.matmul(xat[oc % 2][:, :],
                                      L.wo_w[oc % 2][:, ic * 128:(ic + 1) * 128],
                                      L.ao_a[:, ic * TQ:(ic + 1) * TQ],
                                      start=(ic == 0), stop=(ic == NIC - 1))
                    L.pe_d.inc(mm)
                    socs = [oc - 1] if oc >= 1 else []
                    if oc == NOC - 1:
                        socs.append(oc)
                    for soc in socs:
                        t.wait_ge(L.vc.h, tq * NOC + soc + 1)
                        t.wait_ge(L.sqa.h, tq * NOC + soc + 1)
                        t.matmul(sum_p[:, :], p.ones_bf[:, :],
                                 L.xmb[soc % 2][:, :],
                                 start=(soc == 0), stop=(soc == NOC - 1),
                                 skip_group_check=True)
                        L.pst.inc(t.matmul(sq_p[:, :], p.ones_bf[:, :],
                                           L.sqb[soc % 2][:, :],
                                           start=(soc == 0), stop=(soc == NOC - 1),
                                           skip_group_check=True))
                t.wait_ge(L.astat.h, 2 * (tq + 1))
                t.matmul(xat[0][:, :], p.ones1_f[:, :],
                         L.stat[:, 0:TQ], start=True, stop=True)
                L.bmd.inc(t.matmul(xat[1][:, :], p.ones1_f[:, :],
                                   L.stat[:, 2 * TQ:3 * TQ],
                                   start=True, stop=True))

            @blk.vector
            def _(v):
                for oc in range(NOC):
                    gx = tq * NOC + oc
                    v.wait_ge(L.pe_d.h, gx + 1)
                    v.wait_ge(L.xql[oc % 2].h, 16 * (tq * 16 + oc // 2 + 1))
                    L.vadd.inc(v.tensor_tensor(L.xmid[:, oc * TQ:(oc + 1) * TQ],
                                               xat[oc % 2][:, :],
                                               L.xq[oc % 2][:, :],
                                               ALU.add))
                    v.wait_ge(L.vadd.h, gx + 1)
                    if oc >= 2:
                        v.wait_ge(L.pst.h, gx - 1)
                    L.vc.inc(v.tensor_copy(L.xmb[oc % 2][:, :],
                                           L.xmid[:, oc * TQ:(oc + 1) * TQ]))
                v.wait_ge(L.pst.h, NOC * (tq + 1))
                L.vstat.inc(v.tensor_scalar(L.stat[:, 0:TQ], sum_p[:, :],
                                            1.0 / D, None, ALU.mult))
                L.vstat.wait(v)
                L.vstat.inc(v.tensor_scalar(L.stat[:, TQ:2 * TQ], sq_p[:, :],
                                            1.0 / D, None, ALU.mult))
                L.vstat.wait(v)
                L.vstat.inc(v.tensor_tensor(L.stat[:, 3 * TQ:4 * TQ],
                                            L.stat[:, 0:TQ], L.stat[:, 0:TQ],
                                            ALU.mult))
                L.vstat.wait(v)
                L.vstat.inc(v.tensor_tensor(L.stat[:, TQ:2 * TQ],
                                            L.stat[:, TQ:2 * TQ],
                                            L.stat[:, 3 * TQ:4 * TQ],
                                            ALU.subtract))
                v.wait_ge(L.mrc.h, 2 * (tq + 1))
                for oc in range(NOC):
                    par = oc % 2
                    L.dap.inc(v.tensor_tensor(L.tmp[par][:, :],
                                              L.xmid[:, oc * TQ:(oc + 1) * TQ],
                                              L.mrb[:, 0:TQ], ALU.subtract))
                    L.dap.wait(v)
                    L.x1v.inc(v.tensor_tensor(L.x1b[:, oc * TQ:(oc + 1) * TQ],
                                              L.tmp[par][:, :],
                                              L.mrb[:, TQ:2 * TQ], ALU.mult))

            @blk.scalar
            def _(scl):
                for oc in range(NOC):
                    gx = tq * NOC + oc
                    scl.wait_ge(L.vadd.h, gx + 1)
                    if oc >= 2:
                        scl.wait_ge(L.pst.h, gx - 1)
                    L.sqa.inc(scl.activation(L.sqb[oc % 2][:, :],
                                             L.xmid[:, oc * TQ:(oc + 1) * TQ],
                                             AF.Square))
                scl.wait_ge(L.vstat.h, 4 * (tq + 1))
                L.astat.inc(scl.activation(L.stat[:, 3 * TQ:4 * TQ],
                                           L.stat[:, TQ:2 * TQ], AF.Ln,
                                           bias=p.eps_sb[0:1, 0:1]))
                L.astat.wait(scl)
                L.astat.inc(scl.activation(L.stat[:, 2 * TQ:3 * TQ],
                                           L.stat[:, 3 * TQ:4 * TQ], AF.Exp,
                                           scale=-0.5))
                scl.wait_ge(L.bmd.h, tq + 1)
                L.mrc.inc(scl.copy(L.mrb[:, 0:TQ], xat[0][:, :]))
                L.mrc.inc(scl.copy(L.mrb[:, TQ:2 * TQ], xat[1][:, :]))


def emit_e_quarter(p, tq, L):
    nc = p.nc
    with contextlib.ExitStack() as s:
        L.bw_w = [s.enter_context(nc.sbuf_tensor(f"e_bw{tq}_{i}",
                                                 [128, NIC * 128], BF16))
                  for i in range(2)]
        L.bt_w = [s.enter_context(nc.sbuf_tensor(f"e_bt{tq}_{i}",
                                                 [128, ERC * 128], BF16))
                  for i in range(2)]
        L.g_sb = s.enter_context(nc.sbuf_tensor(f"e_g{tq}", [128, ERC * TQ], BF16))
        L.art_sb = s.enter_context(nc.sbuf_tensor(f"e_art{tq}", [128, NIC * 80],
                                                  BF16))
        L.xt_sb = s.enter_context(nc.sbuf_tensor(f"e_xt{tq}", [128, D], F32))
        L.sqt = s.enter_context(nc.sbuf_tensor(f"e_sqt{tq}", [128, D], BF16))
        L.out_sb = s.enter_context(nc.sbuf_tensor(f"e_out{tq}", [128, D], F32))
        ar_p = s.enter_context(nc.psum_tensor(f"e_ar{tq}", [80, TQ], F32))
        ffn_p = [s.enter_context(nc.psum_tensor(f"e_ffn{tq}_{i}", [128, TQ], F32))
                 for i in range(2)]
        gb_p = [s.enter_context(nc.psum_tensor(f"e_gb{tq}_{i}", [128, TQ], F32))
                for i in range(2)]
        tp_p = [s.enter_context(nc.psum_tensor(f"e_tp{tq}_{i}", [128, 128], F32))
                for i in range(2)]
        lt_p = tp_p[0]
        ct_p = tp_p[1]

        with nc.Block() as blk:
            @blk.sync
            def _(sy):
                sy.wait_ge(L.pe_d.h, NOC * (tq + 1))
                sy.wait_ge(L.vadd.h, NOC * (tq + 1))
                for oc in range(NOC):
                    gx = tq * NOC + oc
                    if oc >= 2:
                        sy.wait_ge(L.fpe.h, gx - 1)
                    wsrc = p.bw_bf[:, oc * 128:(oc + 1) * 128]
                    wsrc = wsrc.rearrange("(c p) o -> p c o", p=128)
                    wdst = L.bw_w[oc % 2][:, :].rearrange("q (c o) -> q c o",
                                                          c=NIC)
                    L.bwl[oc % 2].inc(sy.dma_start(out=wdst, in_=wsrc), 16)
                    bsrc = p.bt_bf[:, oc * 128:(oc + 1) * 128]
                    bsrc = bsrc.rearrange("(c p) o -> p c o", p=128)
                    bdst = L.bt_w[oc % 2][:, :].rearrange("q (c o) -> q c o",
                                                          c=ERC)
                    L.btl[oc % 2].inc(sy.dma_start(out=bdst, in_=bsrc), 16)

            @blk.gpsimd
            def _(g):
                g.wait_ge(L.pe_d.h, NOC * (tq + 1))
                g.wait_ge(L.vadd.h, NOC * (tq + 1))
                asrc = p.art.rearrange("(c p) o -> p c o", p=128)
                adst = L.art_sb[:, :].rearrange("q (c o) -> q c o", c=NIC)
                L.artl.inc(g.dma_start(out=adst, in_=asrc), 16)
                for tb in range(4):
                    k = 4 * tq + tb
                    g.wait_ge(L.l2v.h, k + 1)
                    L.outst.inc(g.dma_start(
                        out=p.out[tq * TQ + tb * 128:tq * TQ + (tb + 1) * 128, :],
                        in_=L.out_sb[:, :]), 16)
                g.wait_ge(L.outst.h, L.outst.n)

            @blk.tensor
            def _(t):
                if tq == 0:
                    t.wait_ge(p.a_sem.h, 16 * 9)
                t.wait_ge(L.artl.h, 16 * (tq + 1))
                for ic in range(NIC):
                    mm = t.matmul(ar_p[:, :],
                                  L.art_sb[:, ic * 80:(ic + 1) * 80],
                                  L.x1b[:, ic * TQ:(ic + 1) * TQ],
                                  start=(ic == 0), stop=(ic == NIC - 1))
                L.arp.inc(mm)
                t.wait_ge(L.arv.h, 2 * (tq + 1))
                for tb in range(4):
                    if tb > 0:
                        t.wait_ge(L.lte.h, 4 * tq + tb)
                    L.ltp.inc(t.transpose(lt_p[:, 0:16],
                                          L.lg_sb[:, tb * 128:(tb + 1) * 128],
                                          p.id_f[0:16, 0:16]))
                t.wait_ge(L.tch.h, tq + 1)
                for tb in range(4):
                    if tb > 0:
                        t.wait_ge(L.cbf_s.h, 4 * tq + tb)
                    L.ctp.inc(t.transpose(ct_p[0:16, :],
                                          L.cbt[:, tb * 16:(tb + 1) * 16],
                                          p.id_f[:, :]))
                t.wait_ge(L.cbf_s.h, 4 * (tq + 1))
                for c in range(ERC):
                    if c >= 2:
                        t.wait_ge(L.gv.h, tq * ERC + c - 1)
                    L.gbm.inc(t.matmul(gb_p[c % 2][:, :],
                                       p.sel[:, c * 128:(c + 1) * 128],
                                       cb_rhs(L), start=True, stop=True))
                for oc in range(NOC):
                    gx = tq * NOC + oc
                    t.wait_ge(L.bwl[oc % 2].h, 16 * (tq * 16 + oc // 2 + 1))
                    t.wait_ge(L.btl[oc % 2].h, 16 * (tq * 16 + oc // 2 + 1))
                    if oc >= 2:
                        t.wait_ge(L.x2v.h, gx - 1)
                    for ic in range(NIC):
                        t.matmul(ffn_p[oc % 2][:, :],
                                 L.bw_w[oc % 2][:, ic * 128:(ic + 1) * 128],
                                 L.x1b[:, ic * TQ:(ic + 1) * TQ],
                                 start=(ic == 0), stop=False,
                                 skip_group_check=True)
                    t.wait_ge(L.gv.h, ERC * (tq + 1))
                    for c in range(ERC):
                        t.matmul(ffn_p[oc % 2][:, :],
                                 L.bt_w[oc % 2][:, c * 128:(c + 1) * 128],
                                 L.g_sb[:, c * TQ:(c + 1) * TQ],
                                 start=False, stop=False,
                                 skip_group_check=True)
                    L.fpe.inc(t.matmul(ffn_p[oc % 2][:, :],
                                       p.bb_bf[0:1, oc * 128:(oc + 1) * 128],
                                       p.ones512_bf[0:1, :],
                                       start=False, stop=True,
                                       skip_group_check=True))
                for tb in range(4):
                    for oc in range(NOC):
                        k = tb * NOC + oc
                        t.wait_ge(L.x2v.h, tq * NOC + oc + 1)
                        if k >= 2:
                            t.wait_ge(L.xtc.h, tq * 128 + k - 1)
                        L.tps.inc(t.transpose(
                            tp_p[k % 2][:, :],
                            L.xmid[:, oc * TQ + tb * 128:oc * TQ + (tb + 1) * 128],
                            p.id_f[:, :]))

            @blk.scalar
            def _(scl):
                scl.wait_ge(L.arp.h, tq + 1)
                scl.copy(L.rh_rep[0:64, :], ar_p[0:64, :])
                L.arv.inc(scl.copy(L.rh_rep[64:128, :], ar_p[0:64, :]))
                L.arv.inc(scl.copy(L.lg_sb[:, :], ar_p[64:80, :]))
                scl.wait_ge(L.vm0.h, tq + 1)
                for tb in range(4):
                    ai = scl.activation(L.r16a[:, tb * 16:(tb + 1) * 16],
                                        L.lgt[:, tb * 16:(tb + 1) * 16],
                                        AF.Exp, bias=L.c2[:, tb:tb + 1])
                L.aex.inc(ai)
                for tb in range(4):
                    scl.wait_ge(L.ctp.h, 4 * tq + tb + 1)
                    L.cbf_s.inc(scl.copy(L.cb_f[:, tb * 128:(tb + 1) * 128],
                                         ct_p[0:16, :]))
                for tb in range(4):
                    k = 4 * tq + tb
                    for oc in range(NOC):
                        kk = tb * NOC + oc
                        scl.wait_ge(L.tps.h, tq * 128 + kk + 1)
                        if oc == 0:
                            scl.wait_ge(L.l2v.h, k)
                        L.xtc.inc(scl.copy(L.xt_sb[:, oc * 128:(oc + 1) * 128],
                                           tp_p[kk % 2][:, :]))
                    scl.wait_ge(L.l2s.h, k)
                    L.l2a.inc(scl.activation(L.sqt[:, :], L.xt_sb[:, :],
                                             AF.Square))
                    scl.wait_ge(L.l2var.h, k + 1)
                    L.l2a.inc(scl.activation(L.st128[:, 6:7], L.st128[:, 5:6],
                                             AF.Ln, bias=p.eps_sb[:, 0:1]))
                    L.l2a.wait(scl)
                    L.l2a.inc(scl.activation(L.st128[:, 7:8], L.st128[:, 6:7],
                                             AF.Exp, scale=-0.5))

            @blk.vector
            def _(v):
                for tb in range(4):
                    v.wait_ge(L.ltp.h, 4 * tq + tb + 1)
                    if tq > 0 and tb == 0:
                        v.wait_ge(L.ctp.h, 4 * tq)
                    L.lte.inc(v.tensor_copy(L.lgt[:, tb * 16:(tb + 1) * 16],
                                            lt_p[:, 0:16]))
                v.wait_ge(L.lte.h, 4 * (tq + 1))
                for tb in range(4):
                    cv = v.reduce_max(L.c1[:, tb:tb + 1],
                                      L.lgt[:, tb * 16:(tb + 1) * 16], AX.X)
                L.vch.inc(cv)
                L.vch.wait(v)
                L.vm0.inc(v.tensor_scalar(L.c2[:, 0:4], L.c1[:, 0:4], -1.0,
                                          None, ALU.mult))
                v.wait_ge(L.aex.h, tq + 1)

                def tiles(buf):
                    return [buf[:, tb * 16:(tb + 1) * 16] for tb in range(4)]

                ra, rb_, ct = tiles(L.r16a), tiles(L.r16b), tiles(L.cbt)
                for tb in range(4):
                    v.reduce_sum(L.c1[:, tb:tb + 1], ra[tb], AX.X)
                for tb in range(4):
                    cv = v.reduce_max(L.c2[:, tb:tb + 1], ra[tb], AX.X)
                L.vch.inc(cv)
                L.vch.wait(v)
                for tb in range(4):
                    cv = v.tensor_scalar(rb_[tb], ra[tb], L.c2[:, tb:tb + 1],
                                         -1e9, ALU.is_ge, ALU.mult)
                L.vch.inc(cv)
                L.vch.wait(v)
                for tb in range(4):
                    cv = v.tensor_tensor(rb_[tb], ra[tb], rb_[tb], ALU.add)
                L.vch.inc(cv)
                L.vch.wait(v)
                for tb in range(4):
                    cv = v.reduce_max(L.c3[:, tb:tb + 1], rb_[tb], AX.X)
                L.vch.inc(cv)
                L.vch.wait(v)
                for tb in range(4):
                    v.tensor_scalar(rb_[tb], ra[tb], L.c3[:, tb:tb + 1],
                                    None, ALU.is_ge)
                L.vch.inc(v.tensor_scalar(L.c1[:, 0:4], L.c1[:, 0:4], 1e-6,
                                          None, ALU.mult))
                L.vch.wait(v)
                L.vch.inc(v.tensor_tensor(L.c2[:, 0:4], L.c2[:, 0:4],
                                          L.c3[:, 0:4], ALU.add))
                L.vch.wait(v)
                L.vch.inc(v.tensor_tensor(L.c1[:, 0:4], L.c1[:, 0:4],
                                          L.c2[:, 0:4], ALU.add))
                L.vch.wait(v)
                L.vch.inc(v.reciprocal(L.c2[:, 0:4], L.c1[:, 0:4]))
                L.vch.wait(v)
                for tb in range(4):
                    cv = v.tensor_mul(ct[tb], ra[tb], rb_[tb])
                L.vch.inc(cv)
                L.vch.wait(v)
                for tb in range(4):
                    cv = v.tensor_scalar(ct[tb], ct[tb], L.c2[:, tb:tb + 1],
                                         None, ALU.mult)
                L.tch.inc(cv)
                for c in range(ERC):
                    v.wait_ge(L.gbm.h, tq * ERC + c + 1)
                    L.gv.inc(v.tensor_tensor(L.g_sb[:, c * TQ:(c + 1) * TQ],
                                             gb_p[c % 2][:, :],
                                             L.rh_rep[:, :], ALU.mult))
                for oc in range(NOC):
                    gx = tq * NOC + oc
                    v.wait_ge(L.fpe.h, gx + 1)
                    L.x2v.inc(v.tensor_tensor(L.xmid[:, oc * TQ:(oc + 1) * TQ],
                                              ffn_p[oc % 2][:, :],
                                              L.x1b[:, oc * TQ:(oc + 1) * TQ],
                                              ALU.add))
                for tb in range(4):
                    k = 4 * tq + tb
                    v.wait_ge(L.xtc.h, tq * 128 + (tb + 1) * NOC)
                    L.vch2.inc(v.reduce_sum(L.st128[:, 0:1], L.xt_sb[:, :], AX.X))
                    L.vch2.wait(v)
                    v.wait_ge(L.l2a.h, 3 * k + 1)
                    L.l2s.inc(v.reduce_sum(L.st128[:, 1:2], L.sqt[:, :], AX.X))
                    L.l2s.wait(v)
                    L.vch2.inc(v.tensor_scalar(L.st128[:, 2:3], L.st128[:, 0:1],
                                               1.0 / D, None, ALU.mult))
                    L.vch2.wait(v)
                    L.vch2.inc(v.tensor_scalar(L.st128[:, 3:4], L.st128[:, 1:2],
                                               1.0 / D, None, ALU.mult))
                    L.vch2.wait(v)
                    L.vch2.inc(v.tensor_tensor(L.st128[:, 4:5], L.st128[:, 2:3],
                                               L.st128[:, 2:3], ALU.mult))
                    L.vch2.wait(v)
                    L.l2var.inc(v.tensor_tensor(L.st128[:, 5:6], L.st128[:, 3:4],
                                                L.st128[:, 4:5], ALU.subtract))
                    v.wait_ge(L.l2a.h, 3 * k + 3)
                    if k > 0:
                        v.wait_ge(L.outst.h, 16 * k)
                    L.l2v.inc(v.tensor_scalar(L.out_sb[:, :], L.xt_sb[:, :],
                                              L.st128[:, 2:3], L.st128[:, 7:8],
                                              ALU.subtract, ALU.mult))


def cb_rhs(L):
    return L.cb_f[:, :]



def emit_dumps(p):
    nc = p.nc
    es = p.es
    ds = Sem(nc, es, "dump_s")
    outs = []
    if "tables" in p.dump:
        outs += [("d_cos", p.cos[:, :], [128, S], F32),
                 ("d_sina", p.sina[:, :], [128, S], F32),
                 ("d_idf", p.id_f[:, :], [128, 128], F32),
                 ("d_sel", p.sel[:, :], [16, ERC * 128], BF16)]
    if "qkv" in p.dump:
        outs += [("d_qf", p.qf, [NPAN, NHO, 128, PAN], BF16),
                 ("d_kf", p.kf, [NPAN, NHO, 128, PAN], BF16),
                 ("d_vf", p.vf, [NPAN, NHO, 128, PAN], BF16),
                 ("d_xbf", p.xbf, [S, D], BF16)]
    if "ao" in p.dump:
        outs += [("d_aok", p.ao_keep, [NHO, 128, TOWN], BF16),
                 ("d_aop", p.ao_part, [NHO, 128, TOWN], BF16)]
    with nc.Block() as blk:
        @blk.gpsimd
        def _(g):
            if "ao" in p.dump:
                g.wait_ge(p.pc2.h, 16)
            for name, src, shape, dt in outs:
                o = nc.dram_tensor(name, shape, dt, kind="ExternalOutput").ap()
                ds.inc(g.dma_start(out=o, in_=src), 16)
    p.final_waits.append((ds, ds.n))


def emit_final_wait(p):
    nc = p.nc
    with nc.Block() as blk:
        @blk.gpsimd
        def _(g):
            for sem, n in p.final_waits:
                g.wait_ge(sem.h, n)



def make_in_maps(inputs):
    x = np.asarray(inputs["L0_attn_hidden_states"])
    wq = np.asarray(inputs["L0_attn_Wq"])
    wk = np.asarray(inputs["L0_attn_Wk"])
    wv = np.asarray(inputs["L0_attn_Wv"])
    wo = np.asarray(inputs["L0_attn_Wo"])
    bw = np.asarray(inputs["L0_ffn_base_w"])
    bbias = np.asarray(inputs["L0_ffn_base_b"])
    A = np.asarray(inputs["L0_ffn_A"])
    B = np.asarray(inputs["L0_ffn_B"])
    rw = np.asarray(inputs["L0_ffn_router_w"])

    woT = np.ascontiguousarray(wo.T)
    woT_rot = np.ascontiguousarray(np.concatenate([woT[2048:], woT[:2048]], axis=0))
    bwT = np.ascontiguousarray(bw.T)
    art = np.ascontiguousarray(np.concatenate([A.T, rw.T], axis=1))
    btT = np.ascontiguousarray(B.transpose(0, 2, 1).reshape(1024, D))
    bb = np.ascontiguousarray(bbias.reshape(1, D))
    invf_c = (10000.0 ** (-(np.arange(128) % 64) / 64.0)).astype(
        np.float32).reshape(128, 1)
    tpos0 = np.arange(S, dtype=np.float32).reshape(1, S)
    tpos1 = np.ascontiguousarray(
        np.concatenate([np.arange(2048, S), np.arange(0, 2048)])
        .astype(np.float32).reshape(1, S))

    in_maps = []
    for c in range(8):
        b, g = c // 2, c % 2
        sl = slice(2048 * g, 2048 * (g + 1))
        xbc = x[b] if g == 0 else np.ascontiguousarray(
            np.concatenate([x[b, 2048:], x[b, :2048]], axis=0))
        in_maps.append({
            "xb": np.ascontiguousarray(xbc),
            "tpos": tpos0 if g == 0 else tpos1,
            "gsel": np.array([[g, 1 - g]], dtype=np.int32),
            "invf": invf_c,
            "wqT": np.ascontiguousarray(wq.T[:, sl]),
            "wkT": np.ascontiguousarray(wk.T[:, sl]),
            "wvT": np.ascontiguousarray(wv.T[:, sl]),
            "woT": woT if g == 0 else woT_rot,
            "bwT": bwT, "bb": bb, "art": art, "btT": btT,
        })
    return in_maps


def kernel(**inputs):
    in_maps = make_in_maps(inputs)
    nc = build()
    res = run_bass_kernel_spmd(nc, in_maps, core_ids=list(range(8)))
    out = np.empty((4, S, D), np.float32)
    for c in range(8):
        b, g = c // 2, c % 2
        out[b, 2048 * g:2048 * (g + 1), :] = res.results[c]["out"]
    return out
